# version 08
# baseline (speedup 1.0000x reference)
import numpy as np
import concourse.bass as bass
import concourse.mybir as mybir
from concourse.bass_utils import run_bass_kernel_spmd
from contextlib import ExitStack

F32 = mybir.dt.float32
F32R = mybir.dt.float32r
BF16 = mybir.dt.bfloat16
AF = mybir.ActivationFunctionType
ALU = mybir.AluOpType
AX = mybir.AxisListType

NCORES = 8
NTOK = 4096
D = 1024
DC = 8
HC = 16
NSUP = 4
SUP = 1024
TT = 8
NAT = 8
AT = 512

PHASES = ("ffn1", "attn", "ffn2")
FFN_PARTS = ("sh", "rt")
SPDUMP = None
B1DBG = None


def DMA(dst, src):
    return lambda e: e.dma_start(out=dst, in_=src)


def MM(out, lhsT, rhs, st, sp):
    return lambda e: e.matmul(out, lhsT, rhs, start=st, stop=sp)


def TRANS(out, in_, ide):
    return lambda e: e.transpose(out, in_, ide)


def ACT(out, in_, func, bias=0.0, scale=1.0, accum=None):
    return lambda e: e.activation(out=out, in_=in_, func=func, bias=bias,
                                  scale=scale, accum_out=accum)


def COPY(out, in_):
    return lambda e: e.tensor_copy(out=out, in_=in_)


def MEMSET(ap, c):
    return lambda e: e.memset(ap, c)


def TT_(out, a, b, op):
    return lambda e: e.tensor_tensor(out=out, in0=a, in1=b, op=op)


def TS(out, in0, s1, op0, s2=None, op1=ALU.bypass):
    return lambda e: e.tensor_scalar(out=out, in0=in0, scalar1=s1, scalar2=s2,
                                     op0=op0, op1=op1)


def TSS(out, in_, s, op):
    return lambda e: e.tensor_single_scalar(out=out, in_=in_, scalar=s, op=op)


def STT(out, in0, s, in1, op0, op1):
    return lambda e: e.scalar_tensor_tensor(out=out, in0=in0, scalar=s, in1=in1,
                                            op0=op0, op1=op1)


def RED(out, in_, op):
    return lambda e: e.tensor_reduce(out=out, in_=in_, axis=AX.X, op=op)


def RECIP(out, in_):
    return lambda e: e.reciprocal(out=out, in_=in_)


def SCAN(out, d0, d1, init):
    return lambda e: e.tensor_tensor_scan(out=out, data0=d0, data1=d1,
                                          initial=init, op0=ALU.mult,
                                          op1=ALU.add)


class B:
    ENGINES = ("sync", "tensor", "vector", "scalar", "gpsimd")

    def __init__(self):
        self.q = {e: [] for e in self.ENGINES}
        from collections import defaultdict
        self.cnt = defaultdict(int)
        for s in ("pe", "vv", "ss", "pl", "cc"):
            self.cnt[s] = 0

    def wait(self, eng, sem, val):
        if val and val > 0:
            self.q[eng].append(("w", sem, val))

    def op(self, eng, fn, inc=None, amt=1):
        v = None
        if inc is not None:
            self.cnt[inc] += amt
            v = self.cnt[inc]
        self.q[eng].append(("o", fn, inc, amt))
        return v

    def barrier(self):
        vals = dict(self.cnt)
        for e in self.ENGINES:
            for s, v in vals.items():
                self.wait(e, s, v)

    def fence(self, eng, sems=("vv", "ss")):
        for s in sems:
            self.wait(eng, s, self.cnt[s])

    def replay(self, name, eng, sems):
        for item in self.q[name]:
            if item[0] == "w":
                eng.wait_ge(sems[item[1]], item[2])
            else:
                ins = item[1](eng)
                if item[2] is not None:
                    ins.then_inc(sems[item[2]], item[3])


class Kern:
    def __init__(self, phases=PHASES):
        self.phases = phases
        nc = self.nc = bass.Bass("TRN2", target_bir_lowering=False, debug=False)
        em = self.em = B()
        ctx = self.ctx = ExitStack()

        def din(name, shape, dt=F32):
            return nc.dram_tensor(name, list(shape), dt, kind="ExternalInput").ap()

        self.x0 = din("x0", [NTOK, D])
        self.wa_in = din("wa_in", [32, 128, 1024], F32R)
        self.wa_1 = din("wa_1", [32, 128, 1024], F32R)
        self.wb_out = din("wb_out", [2, 16, 128, 512], F32)
        self.wb_2 = din("wb_2", [8, 2, 16, 128, 512], F32)
        self.wgwe_i = din("wgwe", [8, 128, 10], F32R)
        self.gbe_i = din("gbe", [1, 10], F32R)
        self.wk_b = din("wk_b", [8, 128, 1024], F32R)
        self.wv_b = din("wv_b", [8, 128, 1024], F32R)
        self.wr_b = din("wr_b", [8, 128, 1024], F32R)
        self.wo_b = din("wo_b", [2, 8, 128, 512], F32R)
        self.cvec = din("cvec", [128, 40])
        self.iota1 = din("iota1", [128, 512])
        self.iota8 = din("iota8", [128, 8])
        self.egid = din("egid", [128, 8])
        self.ide_i = din("ide_in", [128, 128])
        self.ones11_i = din("ones11", [1, 1], F32R)
        self.onesc_i = din("onesc", [1, 128], F32R)
        self.pmask_i = din("pmask", [8, 1], F32R)
        self.pmaskr_i = din("pmaskr", [128, 8])

        self.out_ap = nc.dram_tensor("out", [NTOK, D], F32,
                                     kind="ExternalOutput").ap()
        self.xres = nc.dram_tensor("xres", [NTOK, D], F32).ap()
        self.sp_all = nc.dram_tensor("sp_all", [3, D, NTOK], F32).ap()
        self.ag1_in = nc.dram_tensor("ag1_in", [1, D], F32).ap()
        self.ag1_out = nc.dram_tensor("ag1_out", [8, D], F32).ap()
        self.ag2_in = nc.dram_tensor("ag2_in", [128, 16], F32).ap()
        self.ag2_out = nc.dram_tensor("ag2_out", [8 * 128, 16], F32).ap()
        self.rw_sp = nc.dram_tensor("rw_sp", [D, NTOK], F32).ap()

        sb = lambda n, s, d=F32: ctx.enter_context(nc.sbuf_tensor(n, s, d))
        self.ide = sb("ide", [128, 128])
        self.cv = sb("cv", [128, 40])
        self.dv = sb("dv", [128, 48])
        self.io1 = sb("io1", [128, 512])
        self.io8 = sb("io8", [128, 8])
        self.egid_s = sb("egid_s", [128, 8])
        self.gbe_s = sb("gbe_s", [1, 10], F32R)
        self.ones11 = sb("ones11_s", [1, 1], F32R)
        self.onesc = sb("onesc_s", [1, 128], F32R)
        self.pmask = sb("pmask_s", [8, 1], F32R)
        self.pmaskr = sb("pmaskr_s", [128, 8])
        self.wgwe = sb("wgwe_s", [128, 8, 10], F32R)
        self.stat = sb("stat", [128, 16])
        self.rt = sb("rt_s", [128, 64])
        self.rt_l = sb("rt_l", [128, 8, 10])
        self.wse = sb("wse", [128, 64])
        self.carry_ab = sb("carry_ab", [128, 16])
        self.cst = sb("cst", [128, 2])
        self.carry_y2 = sb("carry_y2", [128, 8])
        self.s0ab = sb("s0ab", [128, 16])
        self.psA = ctx.enter_context(nc.psum_tensor("psA", [128, 2048], F32))
        self.psB = ctx.enter_context(nc.psum_tensor("psB", [128, 2048], F32))

        fsc = ExitStack()
        fb = lambda n, s, d=F32: fsc.enter_context(nc.sbuf_tensor(n, s, d))
        self.yT = fb("yT", [128, 8, 1024], F32R)
        self.hT = fb("hT", [128, 16, 1024], BF16)
        self.oacc = fb("oacc", [128, 8, 1024])
        self.y_scr = fb("y_scr", [128, 2, 1024])
        self.sq_scr = fb("sq_scr", [128, 1024])
        self.sil_scr = fb("sil_scr", [128, 1024])
        self.x_in = fb("x_in", [128, 3, 1024])
        self.xout = fb("xout", [128, 2, 1024])
        self.wa_buf = fb("wa_buf", [128, 2, 2048], F32R)
        self.wb_buf = fb("wb_buf", [128, 4, 512], BF16)
        fsc.close()

        asc = ExitStack()
        ab = lambda n, s, d=F32: asc.enter_context(nc.sbuf_tensor(n, s, d))
        self.y2T = ab("y2T", [128, 8, 513])
        self.xkT = ab("xkT", [128, 8, 512], F32R)
        self.xvT = ab("xvT", [128, 8, 512], F32R)
        self.xrT = ab("xrT", [128, 8, 512], F32R)
        self.wk_s = ab("wk_s", [128, 2, 1024], F32R)
        self.wv_s = ab("wv_s", [128, 2, 1024], F32R)
        self.wr_s = ab("wr_s", [128, 2, 1024], F32R)
        self.dec_rep = ab("dec_rep", [128, 8, 512])
        self.ek_scr = ab("ek_scr", [128, 2, 512])
        self.ekv_scr = ab("ekv_scr", [128, 2, 512])
        self.a_buf = ab("a_buf", [128, 513])
        self.b_buf = ab("b_buf", [128, 513])
        self.nu_scr = ab("nu_scr", [128, 2, 512])
        self.de_scr = ab("de_scr", [128, 2, 512])
        self.rr_scr = ab("rr_scr", [128, 2, 512])
        self.dm_scr = ab("dm_scr", [128, 2, 512])
        self.ax_in = ab("ax_in", [128, 2, 1024])
        self.ay_scr = ab("ay_scr", [128, 1024])
        self.asq_scr = ab("asq_scr", [128, 1024])
        self.ag1s = ab("ag1s", [8, 1024], F32R)
        self.ag2s = ab("ag2s", [128, 8, 16])
        self.prow = ab("prow", [1, 1024])
        self.prowr = ab("prowr", [1, 1024], F32R)
        asc.close()

        bsc = ExitStack()
        bb = lambda n, s, d=F32: bsc.enter_context(nc.sbuf_tensor(n, s, d))
        self.ndr = bb("ndr", [128, 2, 3, 8, 512])
        self.e1t = bb("e1t", [128, 8, 512])
        self.rwT = bb("rwT", [128, 8, 512], F32R)
        self.wo_buf = bb("wo_buf", [128, 4, 512], F32R)
        self.b2_scr = bb("b2_scr", [128, 5, 512])
        self.xo2 = bb("xo2", [128, 2, 1024])
        self.bx_in = bb("bx_in", [128, 2, 1024])
        bsc.close()

    def vw(self):
        self.em.wait("vector", "vv", self.em.cnt["vv"])

    def V(self, fn):
        v = self.em.op("vector", fn, "vv")
        self.em.wait("vector", "vv", v)
        return v

    def vred(self, out, in_, op):
        em = self.em
        v = em.op("vector", RED(out, in_, op), "vv")
        em.wait("vector", "vv", v)

    def vrecip(self, out, in_):
        em = self.em
        v = em.op("vector", RECIP(out, in_), "vv")
        em.wait("vector", "vv", v)

    def dvc(self, i, ch):
        return self.dv[:, i * 8 + ch:i * 8 + ch + 1]

    def cvc(self, i, ch):
        return self.cv[:, i * 8 + ch:i * 8 + ch + 1]

    def preamble(self):
        em = self.em
        loads = [(self.ide[:], self.ide_i[:, :]), (self.cv[:], self.cvec[:, :]),
                 (self.io1[:], self.iota1[:, :]), (self.io8[:], self.iota8[:, :]),
                 (self.egid_s[:], self.egid[:, :]), (self.gbe_s[:], self.gbe_i[:, :]),
                 (self.ones11[:], self.ones11_i[:, :]),
                 (self.onesc[:], self.onesc_i[:, :]),
                 (self.pmask[:], self.pmask_i[:, :]),
                 (self.pmaskr[:], self.pmaskr_i[:, :]),
                 (self.wgwe[:], self.wgwe_i.rearrange("c p j -> p c j"))]
        for dst, src in loads:
            em.op("sync", DMA(dst, src), "aG", 16)
        em.wait("scalar", "aG", em.cnt["aG"])
        em.wait("vector", "aG", em.cnt["aG"])
        dv, cv = self.dv, self.cv
        em.op("scalar", ACT(dv[:, 40:48], cv[:, 0:8], AF.Exp), "ss")
        em.wait("vector", "ss", em.cnt["ss"])
        self.V(TS(dv[:, 0:8], dv[:, 40:48], -1.0, ALU.mult))
        em.wait("scalar", "vv", em.cnt["vv"])
        em.op("scalar", ACT(dv[:, 8:16], dv[:, 0:8], AF.Exp), "ss")
        em.op("scalar", ACT(dv[:, 16:24], cv[:, 8:16], AF.Exp), "ss")
        self.V(TS(dv[:, 40:48], dv[:, 0:8], 512.0, ALU.mult))
        em.wait("scalar", "vv", em.cnt["vv"])
        em.op("scalar", ACT(dv[:, 24:32], dv[:, 40:48], AF.Exp), "ss")
        em.op("vector", MEMSET(dv[:, 32:40], 1.0), "vv")
        em.op("vector", MEMSET(self.cst[:, 0:1], 1e-5), "vv")
        em.op("vector", MEMSET(self.cst[:, 1:2], 1.0), "vv")

    def emit_ln(self, xt, ydst, sq, n=128):
        em, st = self.em, self.stat
        self.vred(st[0:n, 0:1], xt, ALU.add)
        em.op("scalar", ACT(sq[0:n, :], xt, AF.Square, accum=st[0:n, 1:2]), "ss")
        em.wait("vector", "ss", em.cnt["ss"])
        self.V(TS(st[0:n, 2:3], st[0:n, 0:1], 1.0 / D, ALU.mult))
        self.V(TT_(st[0:n, 3:4], st[0:n, 2:3], st[0:n, 2:3], ALU.mult))
        self.V(STT(st[0:n, 4:5], st[0:n, 1:2], 1.0 / D, st[0:n, 3:4],
                   ALU.mult, ALU.subtract))
        em.wait("scalar", "vv", em.cnt["vv"])
        em.op("scalar", ACT(st[0:n, 5:6], st[0:n, 4:5], AF.Ln,
                            bias=self.cst[0:n, 0:1]), "ss")
        em.op("scalar", ACT(st[0:n, 6:7], st[0:n, 5:6], AF.Exp, scale=-0.5),
              "ss")
        em.wait("vector", "ss", em.cnt["ss"])
        self.V(TS(st[0:n, 7:8], st[0:n, 2:3], -1.0, ALU.mult))
        v = em.op("vector", TS(ydst, xt, st[0:n, 7:8], ALU.add,
                               st[0:n, 6:7], ALU.mult), "vv")
        return ("vv", v)

    def emit_routing_mm(self, tt, ev_tp):
        em, L = self.em, self.psB[:, 0:10]
        em.wait("tensor", "vv", ev_tp)
        for dc in range(DC):
            em.op("tensor", MM(L, self.yT[:, dc, tt * 128:(tt + 1) * 128],
                               self.wgwe[:, dc, :], dc == 0, False), None)
        pe = em.op("tensor", MM(L, self.onesc[:], self.gbe_s[:], False, True),
                   "pe")
        em.wait("vector", "pe", pe)
        self.V(COPY(self.rt_l[:, tt, :], L))

    def emit_routing_chain(self, tt):
        em, r = self.em, self.rt
        Lc = self.rt_l[:, tt, :]
        self.V(TT_(r[:, 0:1], Lc[:, 1:2], Lc[:, 0:1], ALU.subtract))
        self.V(TSS(r[:, 1:2], r[:, 0:1], 0.0, ALU.is_gt))
        self.V(TS(r[:, 2:3], r[:, 0:1], -1.0, ALU.mult))
        self.V(TT_(r[:, 2:3], r[:, 2:3], r[:, 0:1], ALU.max))
        em.wait("scalar", "vv", em.cnt["vv"])
        em.op("scalar", ACT(r[:, 3:4], r[:, 2:3], AF.Exp, scale=-1.0), "ss")
        em.wait("vector", "ss", em.cnt["ss"])
        self.V(TS(r[:, 3:4], r[:, 3:4], 1.0, ALU.add))
        self.vrecip(r[:, 3:4], r[:, 3:4])
        self.V(TS(r[:, 8:16], self.egid_s[:], r[:, 1:2], ALU.is_equal))
        self.V(TS(r[:, 16:24], r[:, 8:16], 1.0, ALU.subtract, 1e38, ALU.mult))
        self.V(TT_(r[:, 24:32], Lc[:, 2:10], r[:, 16:24], ALU.add))
        msk = r[:, 24:32]
        self.vred(r[:, 4:5], msk, ALU.max)
        self.V(TS(r[:, 5:6], r[:, 4:5], -1.0, ALU.mult))
        self.V(TS(r[:, 32:40], msk, r[:, 4:5], ALU.is_ge))
        self.V(STT(r[:, 32:40], r[:, 32:40], -1e4, self.io8[:],
                   ALU.mult, ALU.add))
        self.vred(r[:, 6:7], r[:, 32:40], ALU.min)
        self.V(TS(r[:, 6:7], r[:, 6:7], 1e4, ALU.add))
        self.V(TS(r[:, 32:40], self.io8[:], r[:, 6:7], ALU.is_equal))
        self.V(STT(r[:, 40:48], r[:, 32:40], -2e38, msk,
                   ALU.mult, ALU.add))
        self.vred(r[:, 6:7], r[:, 40:48], ALU.max)
        self.V(TS(r[:, 40:48], r[:, 40:48], r[:, 6:7], ALU.is_ge))
        self.V(STT(r[:, 40:48], r[:, 40:48], -1e4, self.io8[:],
                   ALU.mult, ALU.add))
        self.vred(r[:, 6:7], r[:, 40:48], ALU.min)
        self.V(TS(r[:, 6:7], r[:, 6:7], 1e4, ALU.add))
        self.V(TS(r[:, 40:48], self.io8[:], r[:, 6:7], ALU.is_equal))
        self.V(TT_(r[:, 32:40], r[:, 32:40], r[:, 40:48], ALU.add))
        em.wait("scalar", "vv", em.cnt["vv"])
        em.op("scalar", ACT(r[:, 40:48], msk, AF.Exp, bias=r[:, 5:6]), "ss")
        em.wait("vector", "ss", em.cnt["ss"])
        self.vred(r[:, 6:7], r[:, 40:48], ALU.add)
        self.vrecip(r[:, 7:8], r[:, 6:7])
        self.V(TT_(r[:, 7:8], r[:, 7:8], r[:, 3:4], ALU.mult))
        self.V(TS(r[:, 7:8], r[:, 7:8], 0.5, ALU.mult))
        self.V(TT_(r[:, 40:48], r[:, 32:40], r[:, 40:48], ALU.mult))
        self.V(TS(self.wse[:, tt * 8:(tt + 1) * 8], r[:, 40:48],
                  r[:, 7:8], ALU.mult))

    def acc_ap(self, tt):
        ps = self.psA if tt < 4 else self.psB
        return ps[:, (tt % 4) * 512:(tt % 4) * 512 + 512]

    def emit_ffn(self, src, dst, is_ffn1):
        em = self.em
        xg = [None, None, None]
        og = [0, 0]
        for sp in range(NSUP):
            base = sp * SUP
            if sp > 0 and any(k in FFN_PARTS for k in ("statdump", "odump", "hdump", "wsedump", "rtdump")):
                continue
            a4_vv = self.prev_a4_vv
            ev_tp = self.prev_evtp
            for tt in range(TT):
                xb = self.x_in[:, tt % 3, :]
                if xg[tt % 3] is not None:
                    em.wait("sync", xg[tt % 3][0], xg[tt % 3][1])
                xsem = "xL%d" % (tt % 3)
                ld = em.op("sync", DMA(xb, src[base + tt * 128:
                                               base + (tt + 1) * 128, :]),
                           xsem, 16)
                em.wait("vector", xsem, ld)
                em.wait("scalar", xsem, ld)
                em.wait("vector", "pe", self.tp_pe[tt % 2])
                ysl = self.y_scr[:, tt % 2, :]
                yg = self.emit_ln(xb, ysl, self.sq_scr)
                xg[tt % 3] = yg
                em.wait("tensor", yg[0], yg[1])
                em.wait("tensor", "vv", max(ev_tp, a4_vv))
                for dc in range(DC):
                    em.op("tensor", TRANS(self.psA[:, dc * 128:(dc + 1) * 128],
                                          ysl[:, dc * 128:(dc + 1) * 128],
                                          self.ide[:]),
                          "pe" if dc == DC - 1 else None)
                self.tp_pe[tt % 2] = em.cnt["pe"]
                em.wait("vector", "pe", self.tp_pe[tt % 2])
                ev_tp = em.op("vector", COPY(
                    self.yT[:, :, tt * 128:(tt + 1) * 128],
                    self.psA[:, 0:1024].rearrange("p (c t) -> p c t", c=8)), "vv")
                self.emit_routing_mm(tt, ev_tp)
            for tt in range(TT):
                self.emit_routing_chain(tt)
            a0_vv = em.cnt["vv"]
            a0_ss = em.cnt["ss"]
            if "statdump" in FFN_PARTS and sp == 0:
                em.wait("gpsimd", "vv", a0_vv)
                em.wait("gpsimd", "ss", em.cnt["ss"])
                em.op("gpsimd", DMA(dst[0:128, 0:16], self.stat[:]), "gd", 16)
                em.op("gpsimd", DMA(dst[128:256, 0:1024], self.sq_scr[:]),
                      "gd", 16)
                em.op("gpsimd", DMA(dst[256:384, 0:1024],
                                    self.x_in[:, 7 % 3, :]), "gd", 16)
                continue
            if "ydump" in FFN_PARTS:
                em.wait("gpsimd", "vv", a0_vv)
                for tt in range(TT):
                    em.op("gpsimd", DMA(
                        dst[base + tt * 128:base + (tt + 1) * 128, :]
                        .rearrange("p (c t) -> p c t", c=8),
                        self.yT[:, :, tt * 128:(tt + 1) * 128]), "gd", 16)
                continue

            def stage_a(wsrc, floor_pe):
                pe_group = {}
                hmul = {}
                fl_vv, fl_ss = em.cnt["vv"], em.cnt["ss"]
                for hc in range(HC):
                    par = hc % 2
                    em.wait("sync", "pe", pe_group.get(hc - 2, floor_pe))
                    wsem = "wA%d" % par
                    em.op("sync", DMA(self.wa_buf[:, par, 0:1024], wsrc[hc]),
                          wsem, 16)
                    l2 = em.op("sync", DMA(self.wa_buf[:, par, 1024:2048],
                                           wsrc[hc + 16]), wsem, 16)
                    em.wait("tensor", wsem, l2)
                    if hc < 2:
                        em.wait("tensor", "vv", fl_vv)
                        em.wait("tensor", "ss", fl_ss)
                    else:
                        em.wait("tensor", "vv", hmul[hc - 2])
                    ps_z = self.psA if par == 0 else self.psB
                    for half in range(2):
                        for ts2 in range(2):
                            dz = ps_z[:, half * 1024 + ts2 * 512:
                                      half * 1024 + ts2 * 512 + 512]
                            for dc in range(DC):
                                last = (half, ts2, dc) == (1, 1, DC - 1)
                                em.op("tensor", MM(
                                    dz, self.wa_buf[:, par,
                                                    half * 1024 + dc * 128:
                                                    half * 1024 + (dc + 1) * 128],
                                    self.yT[:, dc, ts2 * 512:(ts2 + 1) * 512],
                                    dc == 0, dc == DC - 1),
                                    "pe" if last else None)
                    pe_group[hc] = em.cnt["pe"]
                    em.wait("scalar", "pe", pe_group[hc])
                    em.op("scalar", ACT(self.sil_scr[:], ps_z[:, 0:1024],
                                        AF.Silu), "ss")
                    em.wait("vector", "ss", em.cnt["ss"])
                    em.wait("vector", "pe", pe_group[hc])
                    hmul[hc] = em.op("vector", TT_(self.hT[:, hc, :],
                                                   self.sil_scr[:],
                                                   ps_z[:, 1024:2048],
                                                   ALU.mult), "vv")
                return em.cnt["pe"]

            def stage_b(nsets, wchunk, combine):
                wq = [em.cnt["pe"]] * 4
                cguard = {0: (em.cnt["vv"], em.cnt["ss"]),
                          1: (em.cnt["vv"], em.cnt["ss"])}
                for s_i in range(nsets):
                    for rnd in range(2):
                        half = (2 * s_i + rnd) % 2
                        ps = self.psA if half == 0 else self.psB
                        for hc in range(HC):
                            qi = (s_i * 2 * HC + rnd * HC + hc) % 4
                            bsem = "wB%d" % qi
                            em.wait("gpsimd", "pe", wq[qi])
                            ld = em.op("gpsimd", DMA(self.wb_buf[:, qi, :],
                                                     wchunk(s_i, hc)), bsem, 16)
                            em.wait("tensor", bsem, ld)
                            if hc == 0:
                                em.wait("tensor", "vv", cguard[half][0])
                                em.wait("tensor", "ss", cguard[half][1])
                            for t4 in range(4):
                                tt = rnd * 4 + t4
                                em.op("tensor", MM(
                                    ps[:, t4 * 512:(t4 + 1) * 512],
                                    self.hT[:, hc, tt * 128:(tt + 1) * 128],
                                    self.wb_buf[:, qi, :], hc == 0, hc == HC - 1),
                                    "pe" if t4 == 3 else None)
                            wq[qi] = em.cnt["pe"]
                        combine(s_i, rnd, half, em.cnt["pe"])
                        cguard[half] = (em.cnt["vv"], em.cnt["ss"])
            if "sh" in FFN_PARTS:
                stage_a(self.wa_in, self.prev_pe)

                def comb_sh(s_i, rnd, half, pe):
                    ps = self.psA if half == 0 else self.psB
                    em.wait("scalar", "pe", pe)
                    for t4 in range(4):
                        tt = rnd * 4 + t4
                        em.op("scalar", ACT(
                            self.oacc[:, tt, s_i * 512:(s_i + 1) * 512],
                            ps[:, t4 * 512:(t4 + 1) * 512], AF.Copy,
                            scale=0.5), "ss")
                stage_b(2, lambda s_i, hc: self.wb_out[s_i, hc], comb_sh)
            else:
                em.wait("vector", "pe", em.cnt["pe"])
                for tt in range(TT):
                    em.op("vector", MEMSET(self.oacc[:, tt, :], 0.0), "vv")

            if "rtdump" in FFN_PARTS and sp == 0:
                continue
            if "wsedump" in FFN_PARTS and sp == 0:
                em.wait("gpsimd", "vv", em.cnt["vv"])
                em.op("gpsimd", DMA(dst[0:128, 0:64], self.wse[:]), "gd", 16)
                continue
            if "odump" in FFN_PARTS and sp == 0:
                em.wait("gpsimd", "ss", em.cnt["ss"])
                em.wait("gpsimd", "vv", em.cnt["vv"])
                for tt in range(TT):
                    em.op("gpsimd", DMA(dst[tt * 128:(tt + 1) * 128, :],
                                        self.oacc[:, tt, :]), "gd", 16)
                continue
            if "hdump" in FFN_PARTS and sp == 0:
                em.wait("gpsimd", "vv", em.cnt["vv"])
                for hc in range(HC):
                    em.op("gpsimd", DMA(dst[hc * 128:(hc + 1) * 128, :],
                                        self.hT[:, hc, :]), "gd", 16)
                continue
            if "rt" in FFN_PARTS:
                a2_pe = em.cnt["pe"]
                em.wait("vector", "pe", a2_pe)
                stage_a(self.wa_1, a2_pe)

                def comb_rt(s_i, rnd, half, pe):
                    oh, e = s_i // 8, s_i % 8
                    ps = self.psA if half == 0 else self.psB
                    em.wait("vector", "pe", pe)
                    for t4 in range(4):
                        tt = rnd * 4 + t4
                        sl = self.oacc[:, tt, oh * 512:(oh + 1) * 512]
                        em.op("vector", STT(sl, ps[:, t4 * 512:(t4 + 1) * 512],
                                            self.wse[:, tt * 8 + e:tt * 8 + e + 1],
                                            sl, ALU.mult, ALU.add), "vv")
                stage_b(16, lambda s_i, hc: self.wb_2[s_i % 8, s_i // 8, hc],
                        comb_rt)
            self.prev_a4_vv = em.cnt["vv"]

            em.wait("vector", "ss", em.cnt["ss"])
            for tt in range(TT):
                xb = self.x_in[:, tt % 3, :]
                if xg[tt % 3] is not None:
                    em.wait("sync", xg[tt % 3][0], xg[tt % 3][1])
                xsem = "xL%d" % (tt % 3)
                ld = em.op("sync", DMA(xb, src[base + tt * 128:
                                               base + (tt + 1) * 128, :]),
                           xsem, 16)
                ob = self.xout[:, tt % 2, :]
                em.wait("vector", xsem, ld)
                em.wait("vector", "xS%d" % (tt % 2), og[tt % 2])
                xv = em.op("vector", TT_(ob, self.oacc[:, tt, :], xb,
                                         ALU.add), "vv")
                xg[tt % 3] = ("vv", xv)
                em.wait("gpsimd", "vv", xv)
                og[tt % 2] = em.op("gpsimd", DMA(
                    dst[base + tt * 128:base + (tt + 1) * 128, :], ob),
                    "xS%d" % (tt % 2), 16)
                if is_ffn1 and sp == NSUP - 1 and tt == TT - 1:
                    em.op("gpsimd", DMA(self.ag1_in[:, :],
                                        ob[127:128, :]), "aG", 16)
            self.prev_pe = em.cnt["pe"]
            self.prev_evtp = ev_tp

    def emit_passthrough(self, src, dst):
        em = self.em
        g = [0, 0, 0]
        for i in range(NTOK // 128):
            xb = self.x_in[:, i % 3, :]
            em.wait("sync", "xS%d" % (i % 3), g[i % 3])
            xsem = "xL%d" % (i % 3)
            ld = em.op("sync", DMA(xb, src[i * 128:(i + 1) * 128, :]), xsem, 16)
            em.wait("gpsimd", xsem, ld)
            g[i % 3] = em.op("gpsimd", DMA(dst[i * 128:(i + 1) * 128, :], xb),
                             "xS%d" % (i % 3), 16)
            if i == NTOK // 128 - 1:
                em.op("gpsimd", DMA(self.ag1_in[:, :], xb[127:128, :]),
                      "aG", 16)

    def emit_ag1(self):
        em = self.em
        em.wait("gpsimd", "aG", em.cnt["aG"])
        em.wait("gpsimd", "xS", em.cnt["xS"])
        em.op("gpsimd", lambda e: e.collective_compute(
            "AllGather", ALU.bypass, replica_groups=[list(range(NCORES))],
            ins=[self.ag1_in.opt()], outs=[self.ag1_out.opt()]), "cc", 1)
        em.wait("gpsimd", "cc", em.cnt["cc"])
        ld = em.op("gpsimd", DMA(self.ag1s[:], self.ag1_out[:, :]), "aG", 16)
        em.wait("tensor", "aG", ld)
        for i in range(2):
            em.op("tensor", MM(self.psB[0:1, i * 512:(i + 1) * 512],
                               self.pmask[:], self.ag1s[:, i * 512:(i + 1) * 512],
                               True, True), "pe" if i == 1 else None)
        em.wait("vector", "pe", em.cnt["pe"])
        self.V(COPY(self.prow[:], self.psB[0:1, 0:1024]))
        em.wait("scalar", "vv", em.cnt["vv"])
        yg = self.emit_ln(self.prow[:], self.prowr[:], self.asq_scr, n=1)
        em.wait("tensor", yg[0], yg[1])
        for dc in range(DC):
            em.op("tensor", MM(self.psA[:, 1024 + dc * 8:1024 + (dc + 1) * 8],
                               self.prowr[0:1, dc * 128:(dc + 1) * 128],
                               self.onesc[0:1, 0:8], True, True),
                  "pe" if dc == DC - 1 else None)
        em.wait("vector", "pe", em.cnt["pe"])
        em.op("vector", COPY(
            self.carry_y2[:],
            self.psA[:, 1024:1088].rearrange("p (c e) -> p c e", e=8)[:, :, 0]),
            "vv")
        em.op("vector", MEMSET(self.carry_ab[:], 0.0), "vv")
        self.V(MEMSET(self.dec_rep[:, 0, :], 1.0))
        for ch in range(7, -1, -1):
            em.op("vector", TS(self.dec_rep[:, ch, :], self.dec_rep[:, 0, :],
                               self.dvc(1, ch), ALU.mult), "vv")

    def emit_b1(self):
        em = self.em
        if B1DBG in ("xres", "noag"):
            g = [0, 0]
            for j in range(32):
                xb = self.ax_in[:, j % 2, :]
                em.wait("gpsimd", "gd", g[j % 2])
                em.op("gpsimd", DMA(xb, self.xres[j * 128:(j + 1) * 128, :]),
                      "gd", 16)
                em.wait("gpsimd", "gd", em.cnt["gd"])
                g[j % 2] = em.op("gpsimd", DMA(
                    self.out_ap[j * 128:(j + 1) * 128, :], xb), "gd", 16)
            return
        lng = [None, None]
        tp_ev = 0
        prj_pe = [0, 0]
        ch_ss_par = [0, 0]
        ch_vv_par = [0, 0]
        spg = {}
        for t in range(NAT):
            tb = t * AT
            em.wait("vector", "pl", em.cnt["pl"])
            for tt4 in range(4):
                xb = self.ax_in[:, tt4 % 2, :]
                if lng[tt4 % 2] is not None:
                    em.wait("sync", lng[tt4 % 2][0], lng[tt4 % 2][1])
                xsem = "xL%d" % (tt4 % 2)
                ld = em.op("sync", DMA(xb, self.xres[tb + tt4 * 128:
                                                     tb + (tt4 + 1) * 128, :]),
                           xsem, 16)
                em.wait("vector", xsem, ld)
                em.wait("scalar", xsem, ld)
                em.wait("vector", "pe", self.prev_tp_pe)
                lng[tt4 % 2] = self.emit_ln(xb, self.ay_scr[:], self.asq_scr)
                em.wait("tensor", lng[tt4 % 2][0], lng[tt4 % 2][1])
                for half in range(2):
                    em.wait("tensor", "vv", tp_ev)
                    for d4 in range(4):
                        dc = half * 4 + d4
                        em.op("tensor", TRANS(
                            self.psA[:, 1536 + d4 * 128:1536 + (d4 + 1) * 128],
                            self.ay_scr[:, dc * 128:(dc + 1) * 128], self.ide[:]),
                            "pe" if d4 == 3 else None)
                    self.prev_tp_pe = em.cnt["pe"]
                    em.wait("vector", "pe", em.cnt["pe"])
                    tp_ev = em.op("vector", COPY(
                        self.y2T[:, half * 4:half * 4 + 4,
                                 1 + tt4 * 128:1 + (tt4 + 1) * 128],
                        self.psA[:, 1536:2048].rearrange("p (c t) -> p c t",
                                                         c=4)), "vv")
            self.vw()
            self.V(COPY(self.y2T[:, :, 0:1], self.carry_y2[:]
                        .rearrange("p (c o) -> p c o", o=1)))
            self.V(COPY(self.carry_y2[:]
                        .rearrange("p (c o) -> p c o", o=1),
                        self.y2T[:, :, 512:513]))
            cy_vv = em.cnt["vv"]
            em.wait("gpsimd", "vv", cy_vv)
            em.wait("gpsimd", "pe", prj_pe[0])
            em.wait("gpsimd", "pe", prj_pe[1])
            em.wait("vector", "pe", max(prj_pe))
            mix_vv = 0
            for dc in range(DC):
                pl = em.op("gpsimd", TT_(self.dm_scr[:, dc % 2, :],
                                         self.y2T[:, dc, 1:513],
                                         self.y2T[:, dc, 0:512],
                                         ALU.subtract), "pl")
                em.wait("vector", "pl", pl)
                em.op("vector", STT(self.xkT[:, dc, :], self.dm_scr[:, dc % 2, :],
                                    self.cvc(2, dc), self.y2T[:, dc, 0:512],
                                    ALU.mult, ALU.add), "vv")
                em.op("vector", STT(self.xvT[:, dc, :], self.dm_scr[:, dc % 2, :],
                                    self.cvc(3, dc), self.y2T[:, dc, 0:512],
                                    ALU.mult, ALU.add), "vv")
                mix_vv = em.op("vector", STT(self.xrT[:, dc, :],
                                             self.dm_scr[:, dc % 2, :],
                                             self.cvc(4, dc),
                                             self.y2T[:, dc, 0:512],
                                             ALU.mult, ALU.add), "vv")
                if dc < DC - 2:
                    em.wait("gpsimd", "vv", mix_vv)
            if B1DBG in ("y2t", "xrt"):
                em.wait("gpsimd", "vv", em.cnt["vv"])
                dv8 = self.out_ap.rearrange("(tl x p) o -> tl p x o", tl=8, p=128)[t]
                for c in range(8):
                    s = (self.y2T[:, c, 1:513] if B1DBG == "y2t"
                         else self.xrT[:, c, :])
                    em.op("gpsimd", DMA(
                        dv8[:, c // 2, (c % 2) * 512:(c % 2) * 512 + 512], s),
                        "gd", 16)
                em.wait("vector", "gd", em.cnt["gd"])
                em.wait("scalar", "gd", em.cnt["gd"])
                continue
            for ch in range(8):
                par = ch % 2
                ps = self.psA if par == 0 else self.psB
                em.wait("sync", "pe", prj_pe[par])
                ksem = "wK%d" % par
                em.op("sync", DMA(self.wk_s[:, par, :], self.wk_b[ch]), ksem, 16)
                em.op("sync", DMA(self.wv_s[:, par, :], self.wv_b[ch]), ksem, 16)
                lw = em.op("sync", DMA(self.wr_s[:, par, :], self.wr_b[ch]),
                           ksem, 16)
                em.wait("tensor", ksem, lw)
                em.wait("tensor", "vv", mix_vv)
                em.wait("tensor", "ss", ch_ss_par[par])
                em.wait("tensor", "vv", ch_vv_par[par])
                for wbuf, xsrc, off in ((self.wk_s, self.xkT, 0),
                                        (self.wv_s, self.xvT, 512),
                                        (self.wr_s, self.xrT, 1024)):
                    for dc in range(DC):
                        last = off == 1024 and dc == DC - 1
                        em.op("tensor", MM(ps[:, off:off + 512],
                                           wbuf[:, par, dc * 128:(dc + 1) * 128],
                                           xsrc[:, dc, :], dc == 0, dc == DC - 1),
                              "pe" if last else None)
                prj_pe[par] = em.cnt["pe"]
                em.wait("scalar", "pe", prj_pe[par])
                em.op("scalar", ACT(self.ek_scr[:, par, :], ps[:, 0:512],
                                    AF.Exp), "ss")
                ch_ss_par[par] = em.cnt["ss"]
                em.wait("vector", "ss", ch_ss_par[par])
                em.wait("vector", "pe", prj_pe[par])
                for ki, k in enumerate(("n", "d", "r")):
                    em.wait("vector", "sP%d" % (ki * 2 + par),
                            spg.get((k, par), 0))
                self.V(TT_(self.ekv_scr[:, par, :],
                           self.ek_scr[:, par, :], ps[:, 512:1024], ALU.mult))
                self.V(COPY(self.a_buf[:, 0:1],
                            self.carry_ab[:, ch:ch + 1]))
                self.V(SCAN(self.a_buf[:, 1:513], self.dec_rep[:, ch, :],
                            self.ekv_scr[:, par, :], self.a_buf[:, 0:1]))
                self.V(COPY(self.carry_ab[:, ch:ch + 1],
                            self.a_buf[:, 512:513]))
                self.V(COPY(self.b_buf[:, 0:1],
                            self.carry_ab[:, 8 + ch:9 + ch]))
                self.V(SCAN(self.b_buf[:, 1:513], self.dec_rep[:, ch, :],
                            self.ek_scr[:, par, :], self.b_buf[:, 0:1]))
                self.V(COPY(self.carry_ab[:, 8 + ch:9 + ch],
                            self.b_buf[:, 512:513]))
                em.op("vector", STT(self.nu_scr[:, par, :],
                                    self.ekv_scr[:, par, :], self.dvc(2, ch),
                                    self.a_buf[:, 0:512], ALU.mult, ALU.add),
                      "vv")
                em.op("vector", STT(self.de_scr[:, par, :],
                                    self.ek_scr[:, par, :], self.dvc(2, ch),
                                    self.b_buf[:, 0:512], ALU.mult, ALU.add),
                      "vv")
                self.vw()
                rv = em.op("vector", COPY(self.rr_scr[:, par, :],
                                          ps[:, 1024:1536]), "vv")
                ch_vv_par[par] = rv
                em.wait("gpsimd", "vv", rv)
                spg[("n", par)] = em.op("gpsimd", DMA(
                    self.sp_all[0, ch * 128:(ch + 1) * 128, tb:tb + AT],
                    self.nu_scr[:, par, :]), "sP%d" % par, 16)
                spg[("d", par)] = em.op("gpsimd", DMA(
                    self.sp_all[1, ch * 128:(ch + 1) * 128, tb:tb + AT],
                    self.de_scr[:, par, :]), "sP%d" % (2 + par), 16)
                spg[("r", par)] = em.op("gpsimd", DMA(
                    self.sp_all[2, ch * 128:(ch + 1) * 128, tb:tb + AT],
                    self.rr_scr[:, par, :]), "sP%d" % (4 + par), 16)

    def emit_ag2(self):
        em = self.em
        em.wait("gpsimd", "vv", em.cnt["vv"])
        em.op("gpsimd", DMA(self.ag2_in[:, :], self.carry_ab[:]), "aG", 16)
        em.wait("gpsimd", "aG", em.cnt["aG"])
        for ki in range(6):
            em.wait("gpsimd", "sP%d" % ki, em.cnt["sP%d" % ki])
        em.op("gpsimd", lambda e: e.collective_compute(
            "AllGather", ALU.bypass, replica_groups=[list(range(NCORES))],
            ins=[self.ag2_in.opt()], outs=[self.ag2_out.opt()]), "cc", 1)
        em.wait("gpsimd", "cc", em.cnt["cc"])
        ld = em.op("gpsimd", DMA(
            self.ag2s[:],
            self.ag2_out.rearrange("(r p) c -> p r c", p=128)), "aG", 16)
        em.wait("vector", "aG", ld)
        self.V(TS(self.s0ab[:], self.ag2s[:, 0, :],
                  self.pmaskr[:, 0:1], ALU.mult))
        for rk in range(1, 8):
            self.V(STT(self.s0ab[:], self.ag2s[:, rk, :],
                       self.pmaskr[:, rk:rk + 1], self.s0ab[:],
                       ALU.mult, ALU.add))
        if B1DBG == "s0dump":
            em.wait("gpsimd", "vv", em.cnt["vv"])
            em.op("gpsimd", DMA(self.rw_sp[0:128, 0:16], self.s0ab[:]),
                  "gd", 16)
            em.op("gpsimd", DMA(self.rw_sp[0:128, 16:144],
                                self.ag2s[:].rearrange("p r c -> p (r c)")),
                  "gd", 16)
            em.op("gpsimd", DMA(self.rw_sp[0:128, 144:160], self.carry_ab[:]),
                  "gd", 16)
            g2 = em.cnt["gd"]
            em.wait("vector", "gd", g2)

    def emit_b2(self):
        em = self.em
        for dc in range(DC):
            em.op("vector", TS(self.e1t[:, dc, :], self.io1[:],
                               self.dvc(0, dc), ALU.mult), "vv")
        em.wait("scalar", "vv", em.cnt["vv"])
        for dc in range(DC):
            em.op("scalar", ACT(self.e1t[:, dc, :], self.e1t[:, dc, :],
                                AF.Exp), "ss")
        e1_ss = em.cnt["ss"]
        em.wait("vector", "ss", e1_ss)
        ndr_g = [0, 0]
        ndr_ld = [0, 0]
        prev_op_pe = [0]
        wo_pe = [0] * 4
        xg = [None, None]
        og = [0, 0]
        oproj_done = 0
        for t in range(NAT):
            tb = t * AT
            par = t % 2
            em.wait("sync", "vv", ndr_g[par])
            nsem = "nD%d" % par
            for q in range(3):
                ndr_ld[par] = em.op("sync", DMA(
                    self.ndr[:, par, q, :, :],
                    self.sp_all[q].rearrange("(c p) n -> p c n", p=128)
                    [:, :, tb:tb + AT]), nsem, 16)
            em.wait("vector", nsem, ndr_ld[par])
            em.wait("scalar", nsem, ndr_ld[par])
            em.wait("vector", "pe", prev_op_pe[0])
            for dc in range(DC):
                self.V(TS(self.b2_scr[:, 0, :], self.e1t[:, dc, :],
                          self.dvc(4, dc), ALU.mult))
                em.op("vector", STT(self.b2_scr[:, 1, :], self.b2_scr[:, 0, :],
                                    self.s0ab[:, dc:dc + 1],
                                    self.ndr[:, par, 0, dc, :],
                                    ALU.mult, ALU.add), "vv")
                self.V(STT(self.b2_scr[:, 2, :], self.b2_scr[:, 0, :],
                           self.s0ab[:, 8 + dc:9 + dc],
                           self.ndr[:, par, 1, dc, :],
                           ALU.mult, ALU.add))
                em.op("vector", TSS(self.b2_scr[:, 2, :], self.b2_scr[:, 2, :],
                                    1e-6, ALU.max), "vv")
                em.wait("scalar", "vv", em.cnt["vv"])
                em.op("scalar", ACT(self.b2_scr[:, 3, :],
                                    self.ndr[:, par, 2, dc, :], AF.Exp,
                                    scale=-1.0), "ss")
                em.op("scalar", ACT(self.b2_scr[:, 4, :], self.b2_scr[:, 3, :],
                                    AF.Ln, bias=self.cst[:, 1:2]), "ss")
                em.op("scalar", ACT(self.b2_scr[:, 3, :], self.b2_scr[:, 2, :],
                                    AF.Ln), "ss")
                em.wait("vector", "ss", em.cnt["ss"])
                em.op("vector", TT_(self.b2_scr[:, 4, :], self.b2_scr[:, 4, :],
                                    self.b2_scr[:, 3, :], ALU.add), "vv")
                em.wait("scalar", "vv", em.cnt["vv"])
                em.op("scalar", ACT(self.b2_scr[:, 3, :], self.b2_scr[:, 4, :],
                                    AF.Exp, scale=-1.0), "ss")
                em.wait("vector", "ss", em.cnt["ss"])
                em.op("vector", TT_(self.rwT[:, dc, :], self.b2_scr[:, 1, :],
                                    self.b2_scr[:, 3, :], ALU.mult), "vv")
                if B1DBG == "b2in" and t == 0:
                    self.vw()
                    em.wait("gpsimd", "vv", em.cnt["vv"])
                    em.op("gpsimd", DMA(
                        self.rw_sp[dc * 128:(dc + 1) * 128, 0:512],
                        self.b2_scr[:, 1, :]), "gd", 16)
                    em.op("gpsimd", DMA(
                        self.rw_sp[dc * 128:(dc + 1) * 128, 512:1024],
                        self.b2_scr[:, 2, :]), "gd", 16)
                    em.op("gpsimd", DMA(
                        self.rw_sp[dc * 128:(dc + 1) * 128, 1024:1536],
                        self.b2_scr[:, 3, :]), "gd", 16)
                    em.op("gpsimd", DMA(
                        self.rw_sp[dc * 128:(dc + 1) * 128, 1536:2048],
                        self.ndr[:, par, 1, dc, :]), "gd", 16)
                    em.op("gpsimd", DMA(
                        self.rw_sp[dc * 128:(dc + 1) * 128, 2048:2560],
                        self.ndr[:, par, 0, dc, :]), "gd", 16)
                    gsnap = em.cnt["gd"]
                    em.wait("vector", "gd", gsnap)
                    em.wait("scalar", "gd", gsnap)
            rw_vv = em.cnt["vv"]
            ndr_g[par] = rw_vv
            if B1DBG == "rwt2":
                em.wait("gpsimd", "vv", rw_vv)
                for c in range(8):
                    em.op("gpsimd", DMA(
                        self.rw_sp[c * 128:(c + 1) * 128, tb:tb + AT],
                        self.rwT[:, c, :].bitcast(F32)), "gd", 16)
            if B1DBG == "rwt":
                em.wait("gpsimd", "vv", rw_vv)
                dv8 = self.out_ap.rearrange("(tl x p) o -> tl p x o",
                                            tl=8, p=128)[t]
                for c in range(8):
                    em.op("gpsimd", DMA(
                        dv8[:, c // 2, (c % 2) * 512:(c % 2) * 512 + 512],
                        self.rwT[:, c, :].bitcast(F32)), "gd", 16)
                em.wait("vector", "gd", em.cnt["gd"])
                em.wait("scalar", "gd", em.cnt["gd"])
                continue
            em.op("vector", TT_(self.dv[:, 32:40], self.dv[:, 32:40],
                                self.dv[:, 24:32], ALU.mult), "vv")
            em.wait("tensor", "vv", rw_vv)
            em.wait("tensor", "vv", oproj_done)
            for oh in range(2):
                for ch in range(8):
                    qi = (oh * 8 + ch) % 4
                    em.wait("sync", "pe", wo_pe[qi])
                    osem = "wO%d" % qi
                    ld = em.op("sync", DMA(self.wo_buf[:, qi, :],
                                           self.wo_b[oh, ch]), osem, 16)
                    em.wait("tensor", osem, ld)
                    for t4 in range(4):
                        ps = self.psA if t4 < 2 else self.psB
                        em.op("tensor", MM(
                            ps[:, (t4 % 2) * 1024 + oh * 512:
                               (t4 % 2) * 1024 + oh * 512 + 512],
                            self.rwT[:, ch, t4 * 128:(t4 + 1) * 128],
                            self.wo_buf[:, qi, :], ch == 0, ch == 7),
                            "pe" if t4 == 3 else None)
                    wo_pe[qi] = em.cnt["pe"]
            op_pe = em.cnt["pe"]
            prev_op_pe[0] = op_pe
            for t4 in range(4):
                xb = self.bx_in[:, t4 % 2, :]
                if xg[t4 % 2] is not None:
                    em.wait("sync", "vv", xg[t4 % 2])
                xsem = "xL%d" % (t4 % 2)
                ld = em.op("sync", DMA(xb, self.xres[tb + t4 * 128:
                                                     tb + (t4 + 1) * 128, :]),
                           xsem, 16)
                ps = self.psA if t4 < 2 else self.psB
                em.wait("vector", xsem, ld)
                em.wait("vector", "pe", op_pe)
                em.wait("vector", "xS%d" % (t4 % 2), og[t4 % 2])
                xv = em.op("vector", TT_(
                    self.xo2[:, t4 % 2, :], xb,
                    ps[:, (t4 % 2) * 1024:(t4 % 2) * 1024 + 1024], ALU.add),
                    "vv")
                xg[t4 % 2] = xv
                em.wait("gpsimd", "vv", xv)
                og[t4 % 2] = em.op("gpsimd", DMA(
                    self.xres[tb + t4 * 128:tb + (t4 + 1) * 128, :],
                    self.xo2[:, t4 % 2, :]), "xS%d" % (t4 % 2), 16)
            oproj_done = em.cnt["vv"]

    def build(self):
        em = self.em
        self.prev_pe = 0
        self.prev_a4_vv = 0
        self.prev_evtp = 0
        self.prev_tp_pe = 0
        self.tp_pe = [0, 0]
        dbg = (B1DBG is not None or SPDUMP is not None
               or any(k.endswith("dump") for k in FFN_PARTS))
        self.preamble()
        if "ffn1" in self.phases:
            self.emit_ffn(self.x0, self.xres, True)
        else:
            self.emit_passthrough(self.x0, self.xres)
        em.barrier()
        if "attn" in self.phases and not any(k.endswith("dump") for k in FFN_PARTS):
            if B1DBG != "noag":
                self.emit_ag1()
            self.emit_b1()
            if B1DBG in (None, "rwt", "rwt2", "x2res", "b2in", "s0dump"):
                self.emit_ag2()
                em.barrier()
                self.emit_b2()
            if B1DBG == "x2res":
                em.barrier()
                g2 = [0, 0]
                for j in range(32):
                    xb = self.ax_in[:, j % 2, :]
                    em.wait("gpsimd", "gd", g2[j % 2])
                    em.op("gpsimd", DMA(xb,
                                        self.xres[j * 128:(j + 1) * 128, :]),
                          "gd", 16)
                    em.wait("gpsimd", "gd", em.cnt["gd"])
                    g2[j % 2] = em.op("gpsimd", DMA(
                        self.out_ap[j * 128:(j + 1) * 128, :], xb), "gd", 16)
        if SPDUMP is not None:
            spsrc = self.rw_sp if SPDUMP == 9 else self.sp_all[SPDUMP]
            g = [0, 0]
            for ri in range(8):
                for ci in range(4):
                    j = ri * 4 + ci
                    xb = self.bx_in[:, j % 2, :]
                    em.wait("gpsimd", "gd", g[j % 2])
                    em.op("gpsimd", DMA(
                        xb, spsrc[ri * 128:(ri + 1) * 128,
                                  ci * 1024:(ci + 1) * 1024]), "gd", 16)
                    em.wait("gpsimd", "gd", em.cnt["gd"])
                    g[j % 2] = em.op("gpsimd", DMA(
                        self.out_ap.rearrange("(a c) b -> a (c b)", c=4)
                        [ri * 128:(ri + 1) * 128, ci * 1024:(ci + 1) * 1024],
                        xb), "gd", 16)
        em.barrier()
        if dbg:
            pass
        elif "ffn2" in self.phases:
            self.emit_ffn(self.xres, self.out_ap, False)
        else:
            self.emit_passthrough(self.xres, self.out_ap)
        for s, v in dict(em.cnt).items():
            if s not in ("pe", "vv", "ss", "pl", "cc"):
                em.wait("gpsimd", s, v)

        nc, ctx = self.nc, self.ctx
        sems = {s: ctx.enter_context(nc.semaphore(s)) for s in self.em.cnt}
        block = ctx.enter_context(nc.Block())
        block.sync(lambda e: em.replay("sync", e, sems))
        block.tensor(lambda e: em.replay("tensor", e, sems))
        block.vector(lambda e: em.replay("vector", e, sems))
        block.scalar(lambda e: em.replay("scalar", e, sems))
        block.gpsimd(lambda e: em.replay("gpsimd", e, sems))
        ctx.close()
        return nc


def _host_blocks(i):
    f32 = np.float32
    c = np.ascontiguousarray
    wa_in = c(i["L0_ffn_w_in"].reshape(32, 128, 8, 128)
              .transpose(0, 3, 2, 1).reshape(32, 128, 1024)).astype(f32)
    wa_1 = c(i["L0_ffn_w1"].reshape(32, 128, 8, 128)
             .transpose(0, 3, 2, 1).reshape(32, 128, 1024)).astype(f32)
    wb_out = c(i["L0_ffn_w_out"].reshape(2, 512, 16, 128)
               .transpose(0, 2, 3, 1)).astype(f32)
    wb_2 = c(i["L0_ffn_w2"].reshape(8, 2, 512, 16, 128)
             .transpose(0, 1, 3, 4, 2)).astype(f32)
    G = np.concatenate([np.asarray(i["L0_ffn_wg"]), np.asarray(i["L0_ffn_we"])],
                       axis=0)
    wgwe = c(G.reshape(10, 8, 128).transpose(1, 2, 0)).astype(f32)
    gbe = np.concatenate([np.asarray(i["L0_ffn_gb"]),
                          np.asarray(i["L0_ffn_eb"])])[None, :].astype(f32)
    blk = lambda w: c(np.asarray(w).reshape(8, 128, 8, 128)
                      .transpose(0, 3, 2, 1).reshape(8, 128, 1024)).astype(f32)
    wk_b, wv_b, wr_b = (blk(i["L0_attn_Wk"]), blk(i["L0_attn_Wv"]),
                        blk(i["L0_attn_Wr"]))
    wo_b = c(np.asarray(i["L0_attn_Wo"]).reshape(2, 512, 8, 128)
             .transpose(0, 2, 3, 1)).astype(f32)
    cvec = np.concatenate(
        [np.asarray(i[k], f32).reshape(8, 128).T
         for k in ("L0_attn_time_decay", "L0_attn_time_first",
                   "L0_attn_time_mix_k", "L0_attn_time_mix_v",
                   "L0_attn_time_mix_r")], axis=1)
    cvec = c(cvec).astype(f32)
    return dict(wa_in=wa_in, wa_1=wa_1, wb_out=wb_out, wb_2=wb_2, wgwe=wgwe,
                gbe=gbe, wk_b=wk_b, wv_b=wv_b, wr_b=wr_b, wo_b=wo_b, cvec=cvec)


def _host_consts():
    f32 = np.float32
    return dict(
        iota1=np.tile(np.arange(512, dtype=f32), (128, 1)),
        iota8=np.tile(np.arange(8, dtype=f32), (128, 1)),
        egid=np.tile(np.repeat(np.array([0.0, 1.0], f32), 4), (128, 1)),
        ide_in=np.eye(128, dtype=f32),
        ones11=np.ones((1, 1), f32),
        onesc=np.ones((1, 128), f32),
    )


def kernel(**inputs):
    x = np.ascontiguousarray(np.asarray(inputs["L0_attn_x"], np.float32)
                             .reshape(4 * 8192, 1024))
    shared = _host_blocks(inputs)
    shared.update(_host_consts())
    in_maps = []
    for c in range(NCORES):
        seq, half = divmod(c, 2)
        r0 = seq * 8192 + half * NTOK
        pm = np.zeros((8, 1), np.float32)
        pmr = np.zeros((128, 8), np.float32)
        if half == 1:
            pm[c - 1, 0] = 1.0
            pmr[:, c - 1] = 1.0
        m = dict(shared)
        m["x0"] = np.ascontiguousarray(x[r0:r0 + NTOK])
        m["pmask"] = pm
        m["pmaskr"] = pmr
        in_maps.append(m)
    nc = Kern(PHASES).build()
    res = run_bass_kernel_spmd(nc, in_maps, core_ids=list(range(NCORES)))
    out = np.empty((4, 8192, 1024), np.float32)
    for c in range(NCORES):
        seq, half = divmod(c, 2)
        out[seq, half * NTOK:(half + 1) * NTOK] = res.results[c]["out"]
    return out


# version 12 (final)
# speedup vs baseline: 1.1126x; 1.1126x over previous
import numpy as np
import concourse.bass as bass
import concourse.mybir as mybir
from concourse.bass_utils import run_bass_kernel_spmd
from contextlib import ExitStack

F32 = mybir.dt.float32
F32R = mybir.dt.float32r
BF16 = mybir.dt.bfloat16
AF = mybir.ActivationFunctionType
ALU = mybir.AluOpType
AX = mybir.AxisListType

NCORES = 8
NTOK = 4096
D = 1024
DC = 8
HC = 16
NSUP = 4
SUP = 1024
TT = 8
NAT = 8
AT = 512

PHASES = ("ffn1", "attn", "ffn2")
FFN_PARTS = ("sh", "rt")
SPDUMP = None
B1DBG = None


def DMA(dst, src):
    return lambda e: e.dma_start(out=dst, in_=src)


def MM(out, lhsT, rhs, st, sp):
    return lambda e: e.matmul(out, lhsT, rhs, start=st, stop=sp)


def TRANS(out, in_, ide):
    return lambda e: e.transpose(out, in_, ide)


def ACT(out, in_, func, bias=0.0, scale=1.0, accum=None):
    return lambda e: e.activation(out=out, in_=in_, func=func, bias=bias,
                                  scale=scale, accum_out=accum)


def COPY(out, in_):
    return lambda e: e.tensor_copy(out=out, in_=in_)


def MEMSET(ap, c):
    return lambda e: e.memset(ap, c)


def TT_(out, a, b, op):
    return lambda e: e.tensor_tensor(out=out, in0=a, in1=b, op=op)


def TS(out, in0, s1, op0, s2=None, op1=ALU.bypass):
    return lambda e: e.tensor_scalar(out=out, in0=in0, scalar1=s1, scalar2=s2,
                                     op0=op0, op1=op1)


def TSS(out, in_, s, op):
    return lambda e: e.tensor_single_scalar(out=out, in_=in_, scalar=s, op=op)


def STT(out, in0, s, in1, op0, op1):
    return lambda e: e.scalar_tensor_tensor(out=out, in0=in0, scalar=s, in1=in1,
                                            op0=op0, op1=op1)


def RED(out, in_, op):
    return lambda e: e.tensor_reduce(out=out, in_=in_, axis=AX.X, op=op)


def RECIP(out, in_):
    return lambda e: e.reciprocal(out=out, in_=in_)


def SCAN(out, d0, d1, init):
    return lambda e: e.tensor_tensor_scan(out=out, data0=d0, data1=d1,
                                          initial=init, op0=ALU.mult,
                                          op1=ALU.add)


class B:
    ENGINES = ("sync", "tensor", "vector", "scalar", "gpsimd")

    def __init__(self):
        self.q = {e: [] for e in self.ENGINES}
        from collections import defaultdict
        self.cnt = defaultdict(int)
        for s in ("pe", "vv", "ss", "pl", "cc"):
            self.cnt[s] = 0

    def wait(self, eng, sem, val):
        if val and val > 0:
            self.q[eng].append(("w", sem, val))

    def op(self, eng, fn, inc=None, amt=1):
        v = None
        if inc is not None:
            self.cnt[inc] += amt
            v = self.cnt[inc]
        self.q[eng].append(("o", fn, inc, amt))
        return v

    def barrier(self):
        vals = dict(self.cnt)
        for e in self.ENGINES:
            for s, v in vals.items():
                self.wait(e, s, v)

    def fence(self, eng, sems=("vv", "ss")):
        for s in sems:
            self.wait(eng, s, self.cnt[s])

    def replay(self, name, eng, sems):
        for item in self.q[name]:
            if item[0] == "w":
                eng.wait_ge(sems[item[1]], item[2])
            else:
                ins = item[1](eng)
                if item[2] is not None:
                    ins.then_inc(sems[item[2]], item[3])


class Kern:
    def __init__(self, phases=PHASES):
        self.phases = phases
        nc = self.nc = bass.Bass("TRN2", target_bir_lowering=False, debug=False)
        em = self.em = B()
        ctx = self.ctx = ExitStack()

        def din(name, shape, dt=F32):
            return nc.dram_tensor(name, list(shape), dt, kind="ExternalInput").ap()

        self.x0 = din("x0", [NTOK, D])
        self.wa_in = din("wa_in", [32, 128, 1024], F32R)
        self.wa_1 = din("wa_1", [32, 128, 1024], F32R)
        self.wb_out = din("wb_out", [2, 16, 128, 512], F32)
        self.wb_2 = din("wb_2", [8, 2, 16, 128, 512], F32)
        self.wgwe_i = din("wgwe", [8, 128, 10], F32R)
        self.gbe_i = din("gbe", [1, 10], F32R)
        self.wk_b = din("wk_b", [8, 128, 1024], F32R)
        self.wv_b = din("wv_b", [8, 128, 1024], F32R)
        self.wr_b = din("wr_b", [8, 128, 1024], F32R)
        self.wo_b = din("wo_b", [2, 8, 128, 512], F32R)
        self.cvec = din("cvec", [128, 40])
        self.iota1 = din("iota1", [128, 512])
        self.iota8 = din("iota8", [128, 8])
        self.egid = din("egid", [128, 8])
        self.ide_i = din("ide_in", [128, 128])
        self.ones11_i = din("ones11", [1, 1], F32R)
        self.onesc_i = din("onesc", [1, 128], F32R)
        self.pmask_i = din("pmask", [8, 1], F32R)
        self.pmaskr_i = din("pmaskr", [128, 8])

        self.out_ap = nc.dram_tensor("out", [NTOK, D], F32,
                                     kind="ExternalOutput").ap()
        self.xres = nc.dram_tensor("xres", [NTOK, D], F32).ap()
        self.sp_all = nc.dram_tensor("sp_all", [3, D, NTOK], F32).ap()
        self.ag1_in = nc.dram_tensor("ag1_in", [1, D], F32).ap()
        self.ag1_out = nc.dram_tensor("ag1_out", [8, D], F32).ap()
        self.ag2_in = nc.dram_tensor("ag2_in", [128, 16], F32).ap()
        self.ag2_out = nc.dram_tensor("ag2_out", [8 * 128, 16], F32).ap()
        self.rw_sp = nc.dram_tensor("rw_sp", [D, NTOK], F32).ap()
        self.wboc = nc.dram_tensor("wboc", [2, 16, 128, 512], BF16).ap()
        self.wb2c = nc.dram_tensor("wb2c", [8, 2, 16, 128, 512], BF16).ap()

        sb = lambda n, s, d=F32: ctx.enter_context(nc.sbuf_tensor(n, s, d))
        self.ide = sb("ide", [128, 128])
        self.cv = sb("cv", [128, 40])
        self.dv = sb("dv", [128, 48])
        self.io1 = sb("io1", [128, 512])
        self.io8 = sb("io8", [128, 8])
        self.egid_s = sb("egid_s", [128, 8])
        self.gbe_s = sb("gbe_s", [1, 10], F32R)
        self.ones11 = sb("ones11_s", [1, 1], F32R)
        self.onesc = sb("onesc_s", [1, 128], F32R)
        self.pmask = sb("pmask_s", [8, 1], F32R)
        self.pmaskr = sb("pmaskr_s", [128, 8])
        self.wgwe = sb("wgwe_s", [128, 8, 10], F32R)
        self.stat = sb("stat", [128, 16])
        self.rt = sb("rt_s", [128, 64])
        self.rt_l = sb("rt_l", [128, 8, 10])
        self.wse = sb("wse", [128, 64])
        self.carry_ab = sb("carry_ab", [128, 16])
        self.cst = sb("cst", [128, 2])
        self.cstg = sb("cstg", [128, 2, 4096], BF16)
        self.carry_y2 = sb("carry_y2", [128, 8])
        self.s0ab = sb("s0ab", [128, 16])
        self.psA = ctx.enter_context(nc.psum_tensor("psA", [128, 2048], F32))
        self.psB = ctx.enter_context(nc.psum_tensor("psB", [128, 2048], F32))

        fsc = ExitStack()
        fb = lambda n, s, d=F32: fsc.enter_context(nc.sbuf_tensor(n, s, d))
        self.yT = fb("yT", [128, 8, 1024], F32R)
        self.hT = fb("hT", [128, 16, 1024], BF16)
        self.oacc = fb("oacc", [128, 8, 1024])
        self.y_scr = fb("y_scr", [128, 2, 1024])
        self.sq_scr = fb("sq_scr", [128, 1024])
        self.sil_scr = fb("sil_scr", [128, 1024])
        self.x_in = fb("x_in", [128, 3, 1024])
        self.xout = fb("xout", [128, 2, 1024])
        self.wa_buf = fb("wa_buf", [128, 2, 2048], F32R)
        self.wb_buf = fb("wb_buf", [128, 4, 512], BF16)
        fsc.close()

        asc = ExitStack()
        ab = lambda n, s, d=F32: asc.enter_context(nc.sbuf_tensor(n, s, d))
        self.y2T = ab("y2T", [128, 8, 513])
        self.xkT = ab("xkT", [128, 8, 512], F32R)
        self.xvT = ab("xvT", [128, 8, 512], F32R)
        self.xrT = ab("xrT", [128, 8, 512], F32R)
        self.wk_s = ab("wk_s", [128, 2, 1024], F32R)
        self.wv_s = ab("wv_s", [128, 2, 1024], F32R)
        self.wr_s = ab("wr_s", [128, 2, 1024], F32R)
        self.dec_rep = ab("dec_rep", [128, 8, 512])
        self.ek_scr = ab("ek_scr", [128, 2, 512])
        self.ekv_scr = ab("ekv_scr", [128, 2, 512])
        self.a_buf = ab("a_buf", [128, 513])
        self.b_buf = ab("b_buf", [128, 513])
        self.nu_scr = ab("nu_scr", [128, 2, 512])
        self.de_scr = ab("de_scr", [128, 2, 512])
        self.rr_scr = ab("rr_scr", [128, 2, 512])
        self.dm_scr = ab("dm_scr", [128, 2, 512])
        self.ax_in = ab("ax_in", [128, 2, 1024])
        self.ay_scr = ab("ay_scr", [128, 1024])
        self.asq_scr = ab("asq_scr", [128, 1024])
        self.ag1s = ab("ag1s", [8, 1024], F32R)
        self.ag2s = ab("ag2s", [128, 8, 16])
        self.prow = ab("prow", [1, 1024])
        self.prowr = ab("prowr", [1, 1024], F32R)
        asc.close()

        bsc = ExitStack()
        bb = lambda n, s, d=F32: bsc.enter_context(nc.sbuf_tensor(n, s, d))
        self.ndr = bb("ndr", [128, 2, 3, 8, 512])
        self.e1t = bb("e1t", [128, 8, 512])
        self.rwT = bb("rwT", [128, 8, 512], F32R)
        self.wo_buf = bb("wo_buf", [128, 4, 512], F32R)
        self.b2_scr = bb("b2_scr", [128, 5, 512])
        self.xo2 = bb("xo2", [128, 2, 1024])
        self.bx_in = bb("bx_in", [128, 2, 1024])
        bsc.close()

    def vw(self):
        self.em.wait("vector", "vv", self.em.cnt["vv"])

    def V(self, fn):
        v = self.em.op("vector", fn, "vv")
        self.em.wait("vector", "vv", v)
        return v

    def vred(self, out, in_, op):
        em = self.em
        v = em.op("vector", RED(out, in_, op), "vv")
        em.wait("vector", "vv", v)

    def vrecip(self, out, in_):
        em = self.em
        v = em.op("vector", RECIP(out, in_), "vv")
        em.wait("vector", "vv", v)

    def dvc(self, i, ch):
        return self.dv[:, i * 8 + ch:i * 8 + ch + 1]

    def cvc(self, i, ch):
        return self.cv[:, i * 8 + ch:i * 8 + ch + 1]

    def preamble(self):
        em = self.em
        loads = [(self.ide[:], self.ide_i[:, :]), (self.cv[:], self.cvec[:, :]),
                 (self.io1[:], self.iota1[:, :]), (self.io8[:], self.iota8[:, :]),
                 (self.egid_s[:], self.egid[:, :]), (self.gbe_s[:], self.gbe_i[:, :]),
                 (self.ones11[:], self.ones11_i[:, :]),
                 (self.onesc[:], self.onesc_i[:, :]),
                 (self.pmask[:], self.pmask_i[:, :]),
                 (self.pmaskr[:], self.pmaskr_i[:, :]),
                 (self.wgwe[:], self.wgwe_i.rearrange("c p j -> p c j"))]
        for dst, src in loads:
            em.op("sync", DMA(dst, src), "aG", 16)
        em.wait("scalar", "aG", em.cnt["aG"])
        em.wait("vector", "aG", em.cnt["aG"])
        dv, cv = self.dv, self.cv
        em.op("scalar", ACT(dv[:, 40:48], cv[:, 0:8], AF.Exp), "ss")
        em.wait("vector", "ss", em.cnt["ss"])
        self.V(TS(dv[:, 0:8], dv[:, 40:48], -1.0, ALU.mult))
        em.wait("scalar", "vv", em.cnt["vv"])
        em.op("scalar", ACT(dv[:, 8:16], dv[:, 0:8], AF.Exp), "ss")
        em.op("scalar", ACT(dv[:, 16:24], cv[:, 8:16], AF.Exp), "ss")
        self.V(TS(dv[:, 40:48], dv[:, 0:8], 512.0, ALU.mult))
        em.wait("scalar", "vv", em.cnt["vv"])
        em.op("scalar", ACT(dv[:, 24:32], dv[:, 40:48], AF.Exp), "ss")
        em.op("vector", MEMSET(dv[:, 32:40], 1.0), "vv")
        em.op("vector", MEMSET(self.cst[:, 0:1], 1e-5), "vv")
        em.op("vector", MEMSET(self.cst[:, 1:2], 1.0), "vv")
        stgs = [self.cstg[:, 0, :], self.cstg[:, 1, :]]
        tr_all = []
        for srcf, dstc in ((self.wb_out, self.wboc), (self.wb_2, self.wb2c)):
            n = 1
            for dshape in srcf.shape:
                n *= dshape
            names = " ".join(f"a{k}" for k in range(len(srcf.shape)))
            sview = srcf.rearrange(f"{names} -> ({names})")
            sview = sview.rearrange("(t p c) -> t p c", p=128, c=4096)
            dview = dstc.rearrange(f"{names} -> ({names})")
            dview = dview.rearrange("(t p c) -> t p c", p=128, c=4096)
            for tr in range(n // (128 * 4096)):
                tr_all.append((sview[tr], dview[tr]))
        self.cast_marks = {}
        ti = 0
        for name, n_tr in (("sh", 4), ("rt", 32)):
            for _ in range(n_tr):
                sv, dv = tr_all[ti]
                par = ti % 2
                csem = "wC%d" % par
                if ti >= 2:
                    em.wait("gpsimd", csem, em.cnt[csem])
                ldc = em.op("gpsimd", DMA(stgs[par], sv), csem, 16)
                em.wait("gpsimd", csem, ldc)
                em.op("gpsimd", DMA(dv, stgs[par]), csem, 16)
                ti += 1
            self.cast_marks[name] = (em.cnt["wC0"], em.cnt["wC1"])
        self.cast_wC = (em.cnt["wC0"], em.cnt["wC1"])
        self.cast_waited = set()

    def emit_ln(self, xt, ydst, sq, n=128):
        em, st = self.em, self.stat
        self.vred(st[0:n, 0:1], xt, ALU.add)
        em.op("scalar", ACT(sq[0:n, :], xt, AF.Square, accum=st[0:n, 1:2]), "ss")
        em.wait("vector", "ss", em.cnt["ss"])
        self.V(TS(st[0:n, 2:3], st[0:n, 0:1], 1.0 / D, ALU.mult))
        self.V(TT_(st[0:n, 3:4], st[0:n, 2:3], st[0:n, 2:3], ALU.mult))
        self.V(STT(st[0:n, 4:5], st[0:n, 1:2], 1.0 / D, st[0:n, 3:4],
                   ALU.mult, ALU.subtract))
        em.wait("scalar", "vv", em.cnt["vv"])
        em.op("scalar", ACT(st[0:n, 5:6], st[0:n, 4:5], AF.Ln,
                            bias=self.cst[0:n, 0:1]), "ss")
        em.op("scalar", ACT(st[0:n, 6:7], st[0:n, 5:6], AF.Exp, scale=-0.5),
              "ss")
        em.wait("vector", "ss", em.cnt["ss"])
        self.V(TS(st[0:n, 7:8], st[0:n, 2:3], -1.0, ALU.mult))
        v = em.op("vector", TS(ydst, xt, st[0:n, 7:8], ALU.add,
                               st[0:n, 6:7], ALU.mult), "vv")
        return ("vv", v)

    def emit_routing_mm(self, tt, ev_tp):
        em, L = self.em, self.psB[:, 0:10]
        em.wait("tensor", "vv", ev_tp)
        for dc in range(DC):
            em.op("tensor", MM(L, self.yT[:, dc, tt * 128:(tt + 1) * 128],
                               self.wgwe[:, dc, :], dc == 0, False), None)
        pe = em.op("tensor", MM(L, self.onesc[:], self.gbe_s[:], False, True),
                   "pe")
        em.wait("vector", "pe", pe)
        self.V(COPY(self.rt_l[:, tt, :], L))

    def emit_routing_chain(self, tt):
        em, r = self.em, self.rt
        Lc = self.rt_l[:, tt, :]
        self.V(TT_(r[:, 0:1], Lc[:, 1:2], Lc[:, 0:1], ALU.subtract))
        self.V(TSS(r[:, 1:2], r[:, 0:1], 0.0, ALU.is_gt))
        self.V(TS(r[:, 2:3], r[:, 0:1], -1.0, ALU.mult))
        self.V(TT_(r[:, 2:3], r[:, 2:3], r[:, 0:1], ALU.max))
        em.wait("scalar", "vv", em.cnt["vv"])
        em.op("scalar", ACT(r[:, 3:4], r[:, 2:3], AF.Exp, scale=-1.0), "ss")
        em.wait("vector", "ss", em.cnt["ss"])
        self.V(TS(r[:, 3:4], r[:, 3:4], 1.0, ALU.add))
        self.vrecip(r[:, 3:4], r[:, 3:4])
        self.V(TS(r[:, 8:16], self.egid_s[:], r[:, 1:2], ALU.is_equal))
        self.V(TS(r[:, 16:24], r[:, 8:16], 1.0, ALU.subtract, 1e38, ALU.mult))
        self.V(TT_(r[:, 24:32], Lc[:, 2:10], r[:, 16:24], ALU.add))
        msk = r[:, 24:32]
        self.vred(r[:, 4:5], msk, ALU.max)
        self.V(TS(r[:, 5:6], r[:, 4:5], -1.0, ALU.mult))
        self.V(TS(r[:, 32:40], msk, r[:, 4:5], ALU.is_ge))
        self.V(STT(r[:, 32:40], r[:, 32:40], -1e4, self.io8[:],
                   ALU.mult, ALU.add))
        self.vred(r[:, 6:7], r[:, 32:40], ALU.min)
        self.V(TS(r[:, 6:7], r[:, 6:7], 1e4, ALU.add))
        self.V(TS(r[:, 32:40], self.io8[:], r[:, 6:7], ALU.is_equal))
        self.V(STT(r[:, 40:48], r[:, 32:40], -2e38, msk,
                   ALU.mult, ALU.add))
        self.vred(r[:, 6:7], r[:, 40:48], ALU.max)
        self.V(TS(r[:, 40:48], r[:, 40:48], r[:, 6:7], ALU.is_ge))
        self.V(STT(r[:, 40:48], r[:, 40:48], -1e4, self.io8[:],
                   ALU.mult, ALU.add))
        self.vred(r[:, 6:7], r[:, 40:48], ALU.min)
        self.V(TS(r[:, 6:7], r[:, 6:7], 1e4, ALU.add))
        self.V(TS(r[:, 40:48], self.io8[:], r[:, 6:7], ALU.is_equal))
        self.V(TT_(r[:, 32:40], r[:, 32:40], r[:, 40:48], ALU.add))
        em.wait("scalar", "vv", em.cnt["vv"])
        em.op("scalar", ACT(r[:, 40:48], msk, AF.Exp, bias=r[:, 5:6]), "ss")
        em.wait("vector", "ss", em.cnt["ss"])
        self.vred(r[:, 6:7], r[:, 40:48], ALU.add)
        self.vrecip(r[:, 7:8], r[:, 6:7])
        self.V(TT_(r[:, 7:8], r[:, 7:8], r[:, 3:4], ALU.mult))
        self.V(TS(r[:, 7:8], r[:, 7:8], 0.5, ALU.mult))
        self.V(TT_(r[:, 40:48], r[:, 32:40], r[:, 40:48], ALU.mult))
        self.V(TS(self.wse[:, tt * 8:(tt + 1) * 8], r[:, 40:48],
                  r[:, 7:8], ALU.mult))

    def acc_ap(self, tt):
        ps = self.psA if tt < 4 else self.psB
        return ps[:, (tt % 4) * 512:(tt % 4) * 512 + 512]

    def emit_ffn(self, src, dst, is_ffn1):
        em = self.em
        xg = [None, None, None]
        og = [0, 0]
        for sp in range(NSUP):
            base = sp * SUP
            if sp > 0 and any(k in FFN_PARTS for k in ("statdump", "odump", "hdump", "wsedump", "rtdump")):
                continue
            a4_vv = self.prev_a4_vv
            ev_tp = self.prev_evtp
            for tt in range(TT):
                xb = self.x_in[:, tt % 3, :]
                if xg[tt % 3] is not None:
                    em.wait("sync", xg[tt % 3][0], xg[tt % 3][1])
                xsem = "xL%d" % (tt % 3)
                ld = em.op("sync", DMA(xb, src[base + tt * 128:
                                               base + (tt + 1) * 128, :]),
                           xsem, 16)
                em.wait("vector", xsem, ld)
                em.wait("scalar", xsem, ld)
                em.wait("vector", "pe", self.tp_pe[tt % 2])
                ysl = self.y_scr[:, tt % 2, :]
                yg = self.emit_ln(xb, ysl, self.sq_scr)
                xg[tt % 3] = yg
                em.wait("tensor", yg[0], yg[1])
                em.wait("tensor", "vv", max(ev_tp, a4_vv))
                for dc in range(DC):
                    em.op("tensor", TRANS(self.psA[:, dc * 128:(dc + 1) * 128],
                                          ysl[:, dc * 128:(dc + 1) * 128],
                                          self.ide[:]),
                          "pe" if dc == DC - 1 else None)
                self.tp_pe[tt % 2] = em.cnt["pe"]
                em.wait("vector", "pe", self.tp_pe[tt % 2])
                ev_tp = em.op("vector", COPY(
                    self.yT[:, :, tt * 128:(tt + 1) * 128],
                    self.psA[:, 0:1024].rearrange("p (c t) -> p c t", c=8)), "vv")
                self.emit_routing_mm(tt, ev_tp)
            for tt in range(TT):
                self.emit_routing_chain(tt)
            a0_vv = em.cnt["vv"]
            a0_ss = em.cnt["ss"]
            if "statdump" in FFN_PARTS and sp == 0:
                em.wait("gpsimd", "vv", a0_vv)
                em.wait("gpsimd", "ss", em.cnt["ss"])
                em.op("gpsimd", DMA(dst[0:128, 0:16], self.stat[:]), "gd", 16)
                em.op("gpsimd", DMA(dst[128:256, 0:1024], self.sq_scr[:]),
                      "gd", 16)
                em.op("gpsimd", DMA(dst[256:384, 0:1024],
                                    self.x_in[:, 7 % 3, :]), "gd", 16)
                continue
            if "ydump" in FFN_PARTS:
                em.wait("gpsimd", "vv", a0_vv)
                for tt in range(TT):
                    em.op("gpsimd", DMA(
                        dst[base + tt * 128:base + (tt + 1) * 128, :]
                        .rearrange("p (c t) -> p c t", c=8),
                        self.yT[:, :, tt * 128:(tt + 1) * 128]), "gd", 16)
                continue

            def stage_a(wsrc, floor_pe):
                pe_group = {}
                hmul = {}
                fl_vv, fl_ss = em.cnt["vv"], em.cnt["ss"]
                for hc in range(HC):
                    par = hc % 2
                    em.wait("sync", "pe", pe_group.get(hc - 2, floor_pe))
                    wsem = "wA%d" % par
                    em.op("sync", DMA(self.wa_buf[:, par, 0:1024], wsrc[hc]),
                          wsem, 16)
                    l2 = em.op("sync", DMA(self.wa_buf[:, par, 1024:2048],
                                           wsrc[hc + 16]), wsem, 16)
                    em.wait("tensor", wsem, l2)
                    if hc < 2:
                        em.wait("tensor", "vv", fl_vv)
                        em.wait("tensor", "ss", fl_ss)
                    else:
                        em.wait("tensor", "vv", hmul[hc - 2])
                    ps_z = self.psA if par == 0 else self.psB
                    for half in range(2):
                        for ts2 in range(2):
                            dz = ps_z[:, half * 1024 + ts2 * 512:
                                      half * 1024 + ts2 * 512 + 512]
                            for dc in range(DC):
                                last = (half, ts2, dc) == (1, 1, DC - 1)
                                em.op("tensor", MM(
                                    dz, self.wa_buf[:, par,
                                                    half * 1024 + dc * 128:
                                                    half * 1024 + (dc + 1) * 128],
                                    self.yT[:, dc, ts2 * 512:(ts2 + 1) * 512],
                                    dc == 0, dc == DC - 1),
                                    "pe" if last else None)
                    pe_group[hc] = em.cnt["pe"]
                    em.wait("scalar", "pe", pe_group[hc])
                    em.op("scalar", ACT(self.sil_scr[:], ps_z[:, 0:1024],
                                        AF.Silu), "ss")
                    em.wait("vector", "ss", em.cnt["ss"])
                    em.wait("vector", "pe", pe_group[hc])
                    hmul[hc] = em.op("vector", TT_(self.hT[:, hc, :],
                                                   self.sil_scr[:],
                                                   ps_z[:, 1024:2048],
                                                   ALU.mult), "vv")
                return em.cnt["pe"]

            def stage_b(nsets, wchunk, combine):
                cm = "sh" if nsets == 2 else "rt"
                if cm not in self.cast_waited:
                    self.cast_waited.add(cm)
                    em.wait("sync", "wC0", self.cast_marks[cm][0])
                    em.wait("sync", "wC1", self.cast_marks[cm][1])
                wq = [em.cnt["pe"]] * 4
                cguard = {0: (em.cnt["vv"], em.cnt["ss"]),
                          1: (em.cnt["vv"], em.cnt["ss"])}
                for s_i in range(nsets):
                    for rnd in range(2):
                        half = (2 * s_i + rnd) % 2
                        ps = self.psA if half == 0 else self.psB
                        for hc in range(HC):
                            qi = (s_i * 2 * HC + rnd * HC + hc) % 4
                            bsem = "wB%d" % qi
                            em.wait("sync", "pe", wq[qi])
                            ld = em.op("sync", DMA(self.wb_buf[:, qi, :],
                                                   wchunk(s_i, hc)), bsem, 16)
                            em.wait("tensor", bsem, ld)
                            if hc == 0:
                                em.wait("tensor", "vv", cguard[half][0])
                                em.wait("tensor", "ss", cguard[half][1])
                            for t4 in range(4):
                                tt = rnd * 4 + t4
                                em.op("tensor", MM(
                                    ps[:, t4 * 512:(t4 + 1) * 512],
                                    self.hT[:, hc, tt * 128:(tt + 1) * 128],
                                    self.wb_buf[:, qi, :], hc == 0, hc == HC - 1),
                                    "pe" if t4 == 3 else None)
                            wq[qi] = em.cnt["pe"]
                        combine(s_i, rnd, half, em.cnt["pe"])
                        cguard[half] = (em.cnt["vv"], em.cnt["ss"])
            if "sh" in FFN_PARTS:
                stage_a(self.wa_in, self.prev_pe)

                def comb_sh(s_i, rnd, half, pe):
                    ps = self.psA if half == 0 else self.psB
                    em.wait("scalar", "pe", pe)
                    for t4 in range(4):
                        tt = rnd * 4 + t4
                        em.op("scalar", ACT(
                            self.oacc[:, tt, s_i * 512:(s_i + 1) * 512],
                            ps[:, t4 * 512:(t4 + 1) * 512], AF.Copy,
                            scale=0.5), "ss")
                stage_b(2, lambda s_i, hc: self.wboc[s_i, hc], comb_sh)
            else:
                em.wait("vector", "pe", em.cnt["pe"])
                for tt in range(TT):
                    em.op("vector", MEMSET(self.oacc[:, tt, :], 0.0), "vv")

            if "rtdump" in FFN_PARTS and sp == 0:
                continue
            if "wsedump" in FFN_PARTS and sp == 0:
                em.wait("gpsimd", "vv", em.cnt["vv"])
                em.op("gpsimd", DMA(dst[0:128, 0:64], self.wse[:]), "gd", 16)
                continue
            if "odump" in FFN_PARTS and sp == 0:
                em.wait("gpsimd", "ss", em.cnt["ss"])
                em.wait("gpsimd", "vv", em.cnt["vv"])
                for tt in range(TT):
                    em.op("gpsimd", DMA(dst[tt * 128:(tt + 1) * 128, :],
                                        self.oacc[:, tt, :]), "gd", 16)
                continue
            if "hdump" in FFN_PARTS and sp == 0:
                em.wait("gpsimd", "vv", em.cnt["vv"])
                for hc in range(HC):
                    em.op("gpsimd", DMA(dst[hc * 128:(hc + 1) * 128, :],
                                        self.hT[:, hc, :]), "gd", 16)
                continue
            if "rt" in FFN_PARTS:
                a2_pe = em.cnt["pe"]
                em.wait("vector", "pe", a2_pe)
                stage_a(self.wa_1, a2_pe)

                def comb_rt(s_i, rnd, half, pe):
                    oh, e = s_i // 8, s_i % 8
                    ps = self.psA if half == 0 else self.psB
                    em.wait("vector", "pe", pe)
                    for t4 in range(4):
                        tt = rnd * 4 + t4
                        sl = self.oacc[:, tt, oh * 512:(oh + 1) * 512]
                        em.op("vector", STT(sl, ps[:, t4 * 512:(t4 + 1) * 512],
                                            self.wse[:, tt * 8 + e:tt * 8 + e + 1],
                                            sl, ALU.mult, ALU.add), "vv")
                stage_b(16, lambda s_i, hc: self.wb2c[s_i % 8, s_i // 8, hc],
                        comb_rt)
            self.prev_a4_vv = em.cnt["vv"]

            em.wait("vector", "ss", em.cnt["ss"])
            for tt in range(TT):
                xb = self.x_in[:, tt % 3, :]
                if xg[tt % 3] is not None:
                    em.wait("sync", xg[tt % 3][0], xg[tt % 3][1])
                xsem = "xL%d" % (tt % 3)
                ld = em.op("sync", DMA(xb, src[base + tt * 128:
                                               base + (tt + 1) * 128, :]),
                           xsem, 16)
                ob = self.xout[:, tt % 2, :]
                em.wait("vector", xsem, ld)
                em.wait("vector", "xS%d" % (tt % 2), og[tt % 2])
                xv = em.op("vector", TT_(ob, self.oacc[:, tt, :], xb,
                                         ALU.add), "vv")
                xg[tt % 3] = ("vv", xv)
                em.wait("gpsimd", "vv", xv)
                og[tt % 2] = em.op("gpsimd", DMA(
                    dst[base + tt * 128:base + (tt + 1) * 128, :], ob),
                    "xS%d" % (tt % 2), 16)
                if is_ffn1 and sp == NSUP - 1 and tt == TT - 1:
                    em.op("gpsimd", DMA(self.ag1_in[:, :],
                                        ob[127:128, :]), "aG", 16)
            self.prev_pe = em.cnt["pe"]
            self.prev_evtp = ev_tp

    def emit_passthrough(self, src, dst):
        em = self.em
        g = [0, 0, 0]
        for i in range(NTOK // 128):
            xb = self.x_in[:, i % 3, :]
            em.wait("sync", "xS%d" % (i % 3), g[i % 3])
            xsem = "xL%d" % (i % 3)
            ld = em.op("sync", DMA(xb, src[i * 128:(i + 1) * 128, :]), xsem, 16)
            em.wait("gpsimd", xsem, ld)
            g[i % 3] = em.op("gpsimd", DMA(dst[i * 128:(i + 1) * 128, :], xb),
                             "xS%d" % (i % 3), 16)
            if i == NTOK // 128 - 1:
                em.op("gpsimd", DMA(self.ag1_in[:, :], xb[127:128, :]),
                      "aG", 16)

    def emit_ag1(self):
        em = self.em
        em.wait("gpsimd", "aG", em.cnt["aG"])
        em.wait("gpsimd", "xS", em.cnt["xS"])
        em.op("gpsimd", lambda e: e.collective_compute(
            "AllGather", ALU.bypass, replica_groups=[list(range(NCORES))],
            ins=[self.ag1_in.opt()], outs=[self.ag1_out.opt()]), "cc", 1)
        em.wait("gpsimd", "cc", em.cnt["cc"])
        ld = em.op("gpsimd", DMA(self.ag1s[:], self.ag1_out[:, :]), "aG", 16)
        em.wait("tensor", "aG", ld)
        for i in range(2):
            em.op("tensor", MM(self.psB[0:1, i * 512:(i + 1) * 512],
                               self.pmask[:], self.ag1s[:, i * 512:(i + 1) * 512],
                               True, True), "pe" if i == 1 else None)
        em.wait("vector", "pe", em.cnt["pe"])
        self.V(COPY(self.prow[:], self.psB[0:1, 0:1024]))
        em.wait("scalar", "vv", em.cnt["vv"])
        yg = self.emit_ln(self.prow[:], self.prowr[:], self.asq_scr, n=1)
        em.wait("tensor", yg[0], yg[1])
        for dc in range(DC):
            em.op("tensor", MM(self.psA[:, 1024 + dc * 8:1024 + (dc + 1) * 8],
                               self.prowr[0:1, dc * 128:(dc + 1) * 128],
                               self.onesc[0:1, 0:8], True, True),
                  "pe" if dc == DC - 1 else None)
        em.wait("vector", "pe", em.cnt["pe"])
        em.op("vector", COPY(
            self.carry_y2[:],
            self.psA[:, 1024:1088].rearrange("p (c e) -> p c e", e=8)[:, :, 0]),
            "vv")
        em.op("vector", MEMSET(self.carry_ab[:], 0.0), "vv")
        self.V(MEMSET(self.dec_rep[:, 0, :], 1.0))
        for ch in range(7, -1, -1):
            em.op("vector", TS(self.dec_rep[:, ch, :], self.dec_rep[:, 0, :],
                               self.dvc(1, ch), ALU.mult), "vv")

    def emit_b1(self):
        em = self.em
        if B1DBG in ("xres", "noag"):
            g = [0, 0]
            for j in range(32):
                xb = self.ax_in[:, j % 2, :]
                em.wait("gpsimd", "gd", g[j % 2])
                em.op("gpsimd", DMA(xb, self.xres[j * 128:(j + 1) * 128, :]),
                      "gd", 16)
                em.wait("gpsimd", "gd", em.cnt["gd"])
                g[j % 2] = em.op("gpsimd", DMA(
                    self.out_ap[j * 128:(j + 1) * 128, :], xb), "gd", 16)
            return
        lng = [None, None]
        tp_ev = 0
        prj_pe = [0, 0]
        ch_ss_par = [0, 0]
        ch_vv_par = [0, 0]
        spg = {}
        for t in range(NAT):
            tb = t * AT
            em.wait("vector", "pl", em.cnt["pl"])
            for tt4 in range(4):
                xb = self.ax_in[:, tt4 % 2, :]
                if lng[tt4 % 2] is not None:
                    em.wait("sync", lng[tt4 % 2][0], lng[tt4 % 2][1])
                xsem = "xL%d" % (tt4 % 2)
                ld = em.op("sync", DMA(xb, self.xres[tb + tt4 * 128:
                                                     tb + (tt4 + 1) * 128, :]),
                           xsem, 16)
                em.wait("vector", xsem, ld)
                em.wait("scalar", xsem, ld)
                em.wait("vector", "pe", self.prev_tp_pe)
                lng[tt4 % 2] = self.emit_ln(xb, self.ay_scr[:], self.asq_scr)
                em.wait("tensor", lng[tt4 % 2][0], lng[tt4 % 2][1])
                for half in range(2):
                    em.wait("tensor", "vv", tp_ev)
                    for d4 in range(4):
                        dc = half * 4 + d4
                        em.op("tensor", TRANS(
                            self.psA[:, 1536 + d4 * 128:1536 + (d4 + 1) * 128],
                            self.ay_scr[:, dc * 128:(dc + 1) * 128], self.ide[:]),
                            "pe" if d4 == 3 else None)
                    self.prev_tp_pe = em.cnt["pe"]
                    em.wait("vector", "pe", em.cnt["pe"])
                    tp_ev = em.op("vector", COPY(
                        self.y2T[:, half * 4:half * 4 + 4,
                                 1 + tt4 * 128:1 + (tt4 + 1) * 128],
                        self.psA[:, 1536:2048].rearrange("p (c t) -> p c t",
                                                         c=4)), "vv")
            self.vw()
            self.V(COPY(self.y2T[:, :, 0:1], self.carry_y2[:]
                        .rearrange("p (c o) -> p c o", o=1)))
            self.V(COPY(self.carry_y2[:]
                        .rearrange("p (c o) -> p c o", o=1),
                        self.y2T[:, :, 512:513]))
            cy_vv = em.cnt["vv"]
            em.wait("gpsimd", "vv", cy_vv)
            em.wait("gpsimd", "pe", prj_pe[0])
            em.wait("gpsimd", "pe", prj_pe[1])
            em.wait("vector", "pe", max(prj_pe))
            mix_vv = 0
            for dc in range(DC):
                pl = em.op("gpsimd", TT_(self.dm_scr[:, dc % 2, :],
                                         self.y2T[:, dc, 1:513],
                                         self.y2T[:, dc, 0:512],
                                         ALU.subtract), "pl")
                em.wait("vector", "pl", pl)
                em.op("vector", STT(self.xkT[:, dc, :], self.dm_scr[:, dc % 2, :],
                                    self.cvc(2, dc), self.y2T[:, dc, 0:512],
                                    ALU.mult, ALU.add), "vv")
                em.op("vector", STT(self.xvT[:, dc, :], self.dm_scr[:, dc % 2, :],
                                    self.cvc(3, dc), self.y2T[:, dc, 0:512],
                                    ALU.mult, ALU.add), "vv")
                mix_vv = em.op("vector", STT(self.xrT[:, dc, :],
                                             self.dm_scr[:, dc % 2, :],
                                             self.cvc(4, dc),
                                             self.y2T[:, dc, 0:512],
                                             ALU.mult, ALU.add), "vv")
                if dc < DC - 2:
                    em.wait("gpsimd", "vv", mix_vv)
            if B1DBG in ("y2t", "xrt"):
                em.wait("gpsimd", "vv", em.cnt["vv"])
                dv8 = self.out_ap.rearrange("(tl x p) o -> tl p x o", tl=8, p=128)[t]
                for c in range(8):
                    s = (self.y2T[:, c, 1:513] if B1DBG == "y2t"
                         else self.xrT[:, c, :])
                    em.op("gpsimd", DMA(
                        dv8[:, c // 2, (c % 2) * 512:(c % 2) * 512 + 512], s),
                        "gd", 16)
                em.wait("vector", "gd", em.cnt["gd"])
                em.wait("scalar", "gd", em.cnt["gd"])
                continue
            for ch in range(8):
                par = ch % 2
                ps = self.psA if par == 0 else self.psB
                em.wait("sync", "pe", prj_pe[par])
                ksem = "wK%d" % par
                em.op("sync", DMA(self.wk_s[:, par, :], self.wk_b[ch]), ksem, 16)
                em.op("sync", DMA(self.wv_s[:, par, :], self.wv_b[ch]), ksem, 16)
                lw = em.op("sync", DMA(self.wr_s[:, par, :], self.wr_b[ch]),
                           ksem, 16)
                em.wait("tensor", ksem, lw)
                em.wait("tensor", "vv", mix_vv)
                em.wait("tensor", "ss", ch_ss_par[par])
                em.wait("tensor", "vv", ch_vv_par[par])
                for wbuf, xsrc, off in ((self.wk_s, self.xkT, 0),
                                        (self.wv_s, self.xvT, 512),
                                        (self.wr_s, self.xrT, 1024)):
                    for dc in range(DC):
                        last = off == 1024 and dc == DC - 1
                        em.op("tensor", MM(ps[:, off:off + 512],
                                           wbuf[:, par, dc * 128:(dc + 1) * 128],
                                           xsrc[:, dc, :], dc == 0, dc == DC - 1),
                              "pe" if last else None)
                prj_pe[par] = em.cnt["pe"]
                em.wait("scalar", "pe", prj_pe[par])
                em.op("scalar", ACT(self.ek_scr[:, par, :], ps[:, 0:512],
                                    AF.Exp), "ss")
                ch_ss_par[par] = em.cnt["ss"]
                em.wait("vector", "ss", ch_ss_par[par])
                em.wait("vector", "pe", prj_pe[par])
                for ki, k in enumerate(("n", "d", "r")):
                    em.wait("vector", "sP%d" % (ki * 2 + par),
                            spg.get((k, par), 0))
                self.V(TT_(self.ekv_scr[:, par, :],
                           self.ek_scr[:, par, :], ps[:, 512:1024], ALU.mult))
                self.V(COPY(self.a_buf[:, 0:1],
                            self.carry_ab[:, ch:ch + 1]))
                self.V(SCAN(self.a_buf[:, 1:513], self.dec_rep[:, ch, :],
                            self.ekv_scr[:, par, :], self.a_buf[:, 0:1]))
                self.V(COPY(self.carry_ab[:, ch:ch + 1],
                            self.a_buf[:, 512:513]))
                self.V(COPY(self.b_buf[:, 0:1],
                            self.carry_ab[:, 8 + ch:9 + ch]))
                self.V(SCAN(self.b_buf[:, 1:513], self.dec_rep[:, ch, :],
                            self.ek_scr[:, par, :], self.b_buf[:, 0:1]))
                self.V(COPY(self.carry_ab[:, 8 + ch:9 + ch],
                            self.b_buf[:, 512:513]))
                em.op("vector", STT(self.nu_scr[:, par, :],
                                    self.ekv_scr[:, par, :], self.dvc(2, ch),
                                    self.a_buf[:, 0:512], ALU.mult, ALU.add),
                      "vv")
                em.op("vector", STT(self.de_scr[:, par, :],
                                    self.ek_scr[:, par, :], self.dvc(2, ch),
                                    self.b_buf[:, 0:512], ALU.mult, ALU.add),
                      "vv")
                self.vw()
                rv = em.op("vector", COPY(self.rr_scr[:, par, :],
                                          ps[:, 1024:1536]), "vv")
                ch_vv_par[par] = rv
                em.wait("gpsimd", "vv", rv)
                spg[("n", par)] = em.op("gpsimd", DMA(
                    self.sp_all[0, ch * 128:(ch + 1) * 128, tb:tb + AT],
                    self.nu_scr[:, par, :]), "sP%d" % par, 16)
                spg[("d", par)] = em.op("gpsimd", DMA(
                    self.sp_all[1, ch * 128:(ch + 1) * 128, tb:tb + AT],
                    self.de_scr[:, par, :]), "sP%d" % (2 + par), 16)
                spg[("r", par)] = em.op("gpsimd", DMA(
                    self.sp_all[2, ch * 128:(ch + 1) * 128, tb:tb + AT],
                    self.rr_scr[:, par, :]), "sP%d" % (4 + par), 16)

    def emit_ag2(self):
        em = self.em
        em.wait("gpsimd", "vv", em.cnt["vv"])
        em.op("gpsimd", DMA(self.ag2_in[:, :], self.carry_ab[:]), "aG", 16)
        em.wait("gpsimd", "aG", em.cnt["aG"])
        for ki in range(6):
            em.wait("gpsimd", "sP%d" % ki, em.cnt["sP%d" % ki])
        em.op("gpsimd", lambda e: e.collective_compute(
            "AllGather", ALU.bypass, replica_groups=[list(range(NCORES))],
            ins=[self.ag2_in.opt()], outs=[self.ag2_out.opt()]), "cc", 1)
        em.wait("gpsimd", "cc", em.cnt["cc"])
        ld = em.op("gpsimd", DMA(
            self.ag2s[:],
            self.ag2_out.rearrange("(r p) c -> p r c", p=128)), "aG", 16)
        em.wait("vector", "aG", ld)
        self.V(TS(self.s0ab[:], self.ag2s[:, 0, :],
                  self.pmaskr[:, 0:1], ALU.mult))
        for rk in range(1, 8):
            self.V(STT(self.s0ab[:], self.ag2s[:, rk, :],
                       self.pmaskr[:, rk:rk + 1], self.s0ab[:],
                       ALU.mult, ALU.add))
        if B1DBG == "s0dump":
            em.wait("gpsimd", "vv", em.cnt["vv"])
            em.op("gpsimd", DMA(self.rw_sp[0:128, 0:16], self.s0ab[:]),
                  "gd", 16)
            em.op("gpsimd", DMA(self.rw_sp[0:128, 16:144],
                                self.ag2s[:].rearrange("p r c -> p (r c)")),
                  "gd", 16)
            em.op("gpsimd", DMA(self.rw_sp[0:128, 144:160], self.carry_ab[:]),
                  "gd", 16)
            g2 = em.cnt["gd"]
            em.wait("vector", "gd", g2)

    def emit_b2(self):
        em = self.em
        for dc in range(DC):
            em.op("vector", TS(self.e1t[:, dc, :], self.io1[:],
                               self.dvc(0, dc), ALU.mult), "vv")
        em.wait("scalar", "vv", em.cnt["vv"])
        for dc in range(DC):
            em.op("scalar", ACT(self.e1t[:, dc, :], self.e1t[:, dc, :],
                                AF.Exp), "ss")
        e1_ss = em.cnt["ss"]
        em.wait("vector", "ss", e1_ss)
        ndr_g = [0, 0]
        ndr_ld = [0, 0]
        prev_op_pe = [0]
        wo_pe = [0] * 4
        xg = [None, None]
        og = [0, 0]
        oproj_done = 0
        for t in range(NAT):
            tb = t * AT
            par = t % 2
            em.wait("sync", "vv", ndr_g[par])
            nsem = "nD%d" % par
            for q in range(3):
                ndr_ld[par] = em.op("sync", DMA(
                    self.ndr[:, par, q, :, :],
                    self.sp_all[q].rearrange("(c p) n -> p c n", p=128)
                    [:, :, tb:tb + AT]), nsem, 16)
            em.wait("vector", nsem, ndr_ld[par])
            em.wait("scalar", nsem, ndr_ld[par])
            em.wait("vector", "pe", prev_op_pe[0])
            for dc in range(DC):
                self.V(TS(self.b2_scr[:, 0, :], self.e1t[:, dc, :],
                          self.dvc(4, dc), ALU.mult))
                em.op("vector", STT(self.b2_scr[:, 1, :], self.b2_scr[:, 0, :],
                                    self.s0ab[:, dc:dc + 1],
                                    self.ndr[:, par, 0, dc, :],
                                    ALU.mult, ALU.add), "vv")
                self.V(STT(self.b2_scr[:, 2, :], self.b2_scr[:, 0, :],
                           self.s0ab[:, 8 + dc:9 + dc],
                           self.ndr[:, par, 1, dc, :],
                           ALU.mult, ALU.add))
                em.op("vector", TSS(self.b2_scr[:, 2, :], self.b2_scr[:, 2, :],
                                    1e-6, ALU.max), "vv")
                em.wait("scalar", "vv", em.cnt["vv"])
                em.op("scalar", ACT(self.b2_scr[:, 3, :],
                                    self.ndr[:, par, 2, dc, :], AF.Exp,
                                    scale=-1.0), "ss")
                em.op("scalar", ACT(self.b2_scr[:, 4, :], self.b2_scr[:, 3, :],
                                    AF.Ln, bias=self.cst[:, 1:2]), "ss")
                em.op("scalar", ACT(self.b2_scr[:, 3, :], self.b2_scr[:, 2, :],
                                    AF.Ln), "ss")
                em.wait("vector", "ss", em.cnt["ss"])
                em.op("vector", TT_(self.b2_scr[:, 4, :], self.b2_scr[:, 4, :],
                                    self.b2_scr[:, 3, :], ALU.add), "vv")
                em.wait("scalar", "vv", em.cnt["vv"])
                em.op("scalar", ACT(self.b2_scr[:, 3, :], self.b2_scr[:, 4, :],
                                    AF.Exp, scale=-1.0), "ss")
                em.wait("vector", "ss", em.cnt["ss"])
                em.op("vector", TT_(self.rwT[:, dc, :], self.b2_scr[:, 1, :],
                                    self.b2_scr[:, 3, :], ALU.mult), "vv")
                if B1DBG == "b2in" and t == 0:
                    self.vw()
                    em.wait("gpsimd", "vv", em.cnt["vv"])
                    em.op("gpsimd", DMA(
                        self.rw_sp[dc * 128:(dc + 1) * 128, 0:512],
                        self.b2_scr[:, 1, :]), "gd", 16)
                    em.op("gpsimd", DMA(
                        self.rw_sp[dc * 128:(dc + 1) * 128, 512:1024],
                        self.b2_scr[:, 2, :]), "gd", 16)
                    em.op("gpsimd", DMA(
                        self.rw_sp[dc * 128:(dc + 1) * 128, 1024:1536],
                        self.b2_scr[:, 3, :]), "gd", 16)
                    em.op("gpsimd", DMA(
                        self.rw_sp[dc * 128:(dc + 1) * 128, 1536:2048],
                        self.ndr[:, par, 1, dc, :]), "gd", 16)
                    em.op("gpsimd", DMA(
                        self.rw_sp[dc * 128:(dc + 1) * 128, 2048:2560],
                        self.ndr[:, par, 0, dc, :]), "gd", 16)
                    gsnap = em.cnt["gd"]
                    em.wait("vector", "gd", gsnap)
                    em.wait("scalar", "gd", gsnap)
            rw_vv = em.cnt["vv"]
            ndr_g[par] = rw_vv
            if B1DBG == "rwt2":
                em.wait("gpsimd", "vv", rw_vv)
                for c in range(8):
                    em.op("gpsimd", DMA(
                        self.rw_sp[c * 128:(c + 1) * 128, tb:tb + AT],
                        self.rwT[:, c, :].bitcast(F32)), "gd", 16)
            if B1DBG == "rwt":
                em.wait("gpsimd", "vv", rw_vv)
                dv8 = self.out_ap.rearrange("(tl x p) o -> tl p x o",
                                            tl=8, p=128)[t]
                for c in range(8):
                    em.op("gpsimd", DMA(
                        dv8[:, c // 2, (c % 2) * 512:(c % 2) * 512 + 512],
                        self.rwT[:, c, :].bitcast(F32)), "gd", 16)
                em.wait("vector", "gd", em.cnt["gd"])
                em.wait("scalar", "gd", em.cnt["gd"])
                continue
            em.op("vector", TT_(self.dv[:, 32:40], self.dv[:, 32:40],
                                self.dv[:, 24:32], ALU.mult), "vv")
            em.wait("tensor", "vv", rw_vv)
            em.wait("tensor", "vv", oproj_done)
            for oh in range(2):
                for ch in range(8):
                    qi = (oh * 8 + ch) % 4
                    em.wait("sync", "pe", wo_pe[qi])
                    osem = "wO%d" % qi
                    ld = em.op("sync", DMA(self.wo_buf[:, qi, :],
                                           self.wo_b[oh, ch]), osem, 16)
                    em.wait("tensor", osem, ld)
                    for t4 in range(4):
                        ps = self.psA if t4 < 2 else self.psB
                        em.op("tensor", MM(
                            ps[:, (t4 % 2) * 1024 + oh * 512:
                               (t4 % 2) * 1024 + oh * 512 + 512],
                            self.rwT[:, ch, t4 * 128:(t4 + 1) * 128],
                            self.wo_buf[:, qi, :], ch == 0, ch == 7),
                            "pe" if t4 == 3 else None)
                    wo_pe[qi] = em.cnt["pe"]
            op_pe = em.cnt["pe"]
            prev_op_pe[0] = op_pe
            for t4 in range(4):
                xb = self.bx_in[:, t4 % 2, :]
                if xg[t4 % 2] is not None:
                    em.wait("sync", "vv", xg[t4 % 2])
                xsem = "xL%d" % (t4 % 2)
                ld = em.op("sync", DMA(xb, self.xres[tb + t4 * 128:
                                                     tb + (t4 + 1) * 128, :]),
                           xsem, 16)
                ps = self.psA if t4 < 2 else self.psB
                em.wait("vector", xsem, ld)
                em.wait("vector", "pe", op_pe)
                em.wait("vector", "xS%d" % (t4 % 2), og[t4 % 2])
                xv = em.op("vector", TT_(
                    self.xo2[:, t4 % 2, :], xb,
                    ps[:, (t4 % 2) * 1024:(t4 % 2) * 1024 + 1024], ALU.add),
                    "vv")
                xg[t4 % 2] = xv
                em.wait("gpsimd", "vv", xv)
                og[t4 % 2] = em.op("gpsimd", DMA(
                    self.xres[tb + t4 * 128:tb + (t4 + 1) * 128, :],
                    self.xo2[:, t4 % 2, :]), "xS%d" % (t4 % 2), 16)
            oproj_done = em.cnt["vv"]

    def build(self):
        em = self.em
        self.prev_pe = 0
        self.prev_a4_vv = 0
        self.prev_evtp = 0
        self.prev_tp_pe = 0
        self.tp_pe = [0, 0]
        dbg = (B1DBG is not None or SPDUMP is not None
               or any(k.endswith("dump") for k in FFN_PARTS))
        self.preamble()
        if "ffn1" in self.phases:
            self.emit_ffn(self.x0, self.xres, True)
        else:
            self.emit_passthrough(self.x0, self.xres)
        em.barrier()
        if "attn" in self.phases and not any(k.endswith("dump") for k in FFN_PARTS):
            if B1DBG != "noag":
                self.emit_ag1()
            self.emit_b1()
            if B1DBG in (None, "rwt", "rwt2", "x2res", "b2in", "s0dump"):
                self.emit_ag2()
                em.barrier()
                self.emit_b2()
            if B1DBG == "x2res":
                em.barrier()
                g2 = [0, 0]
                for j in range(32):
                    xb = self.ax_in[:, j % 2, :]
                    em.wait("gpsimd", "gd", g2[j % 2])
                    em.op("gpsimd", DMA(xb,
                                        self.xres[j * 128:(j + 1) * 128, :]),
                          "gd", 16)
                    em.wait("gpsimd", "gd", em.cnt["gd"])
                    g2[j % 2] = em.op("gpsimd", DMA(
                        self.out_ap[j * 128:(j + 1) * 128, :], xb), "gd", 16)
        if SPDUMP is not None:
            spsrc = self.rw_sp if SPDUMP == 9 else self.sp_all[SPDUMP]
            g = [0, 0]
            for ri in range(8):
                for ci in range(4):
                    j = ri * 4 + ci
                    xb = self.bx_in[:, j % 2, :]
                    em.wait("gpsimd", "gd", g[j % 2])
                    em.op("gpsimd", DMA(
                        xb, spsrc[ri * 128:(ri + 1) * 128,
                                  ci * 1024:(ci + 1) * 1024]), "gd", 16)
                    em.wait("gpsimd", "gd", em.cnt["gd"])
                    g[j % 2] = em.op("gpsimd", DMA(
                        self.out_ap.rearrange("(a c) b -> a (c b)", c=4)
                        [ri * 128:(ri + 1) * 128, ci * 1024:(ci + 1) * 1024],
                        xb), "gd", 16)
        em.barrier()
        if dbg:
            pass
        elif "ffn2" in self.phases:
            self.emit_ffn(self.xres, self.out_ap, False)
        else:
            self.emit_passthrough(self.xres, self.out_ap)
        for s, v in dict(em.cnt).items():
            if s not in ("pe", "vv", "ss", "pl", "cc"):
                em.wait("gpsimd", s, v)

        nc, ctx = self.nc, self.ctx
        sems = {s: ctx.enter_context(nc.semaphore(s)) for s in self.em.cnt}
        block = ctx.enter_context(nc.Block())
        block.sync(lambda e: em.replay("sync", e, sems))
        block.tensor(lambda e: em.replay("tensor", e, sems))
        block.vector(lambda e: em.replay("vector", e, sems))
        block.scalar(lambda e: em.replay("scalar", e, sems))
        block.gpsimd(lambda e: em.replay("gpsimd", e, sems))
        ctx.close()
        return nc


def _host_blocks(i):
    f32 = np.float32
    c = np.ascontiguousarray
    wa_in = c(i["L0_ffn_w_in"].reshape(32, 128, 8, 128)
              .transpose(0, 3, 2, 1).reshape(32, 128, 1024)).astype(f32)
    wa_1 = c(i["L0_ffn_w1"].reshape(32, 128, 8, 128)
             .transpose(0, 3, 2, 1).reshape(32, 128, 1024)).astype(f32)
    wb_out = c(i["L0_ffn_w_out"].reshape(2, 512, 16, 128)
               .transpose(0, 2, 3, 1)).astype(f32)
    wb_2 = c(i["L0_ffn_w2"].reshape(8, 2, 512, 16, 128)
             .transpose(0, 1, 3, 4, 2)).astype(f32)
    G = np.concatenate([np.asarray(i["L0_ffn_wg"]), np.asarray(i["L0_ffn_we"])],
                       axis=0)
    wgwe = c(G.reshape(10, 8, 128).transpose(1, 2, 0)).astype(f32)
    gbe = np.concatenate([np.asarray(i["L0_ffn_gb"]),
                          np.asarray(i["L0_ffn_eb"])])[None, :].astype(f32)
    blk = lambda w: c(np.asarray(w).reshape(8, 128, 8, 128)
                      .transpose(0, 3, 2, 1).reshape(8, 128, 1024)).astype(f32)
    wk_b, wv_b, wr_b = (blk(i["L0_attn_Wk"]), blk(i["L0_attn_Wv"]),
                        blk(i["L0_attn_Wr"]))
    wo_b = c(np.asarray(i["L0_attn_Wo"]).reshape(2, 512, 8, 128)
             .transpose(0, 2, 3, 1)).astype(f32)
    cvec = np.concatenate(
        [np.asarray(i[k], f32).reshape(8, 128).T
         for k in ("L0_attn_time_decay", "L0_attn_time_first",
                   "L0_attn_time_mix_k", "L0_attn_time_mix_v",
                   "L0_attn_time_mix_r")], axis=1)
    cvec = c(cvec).astype(f32)
    return dict(wa_in=wa_in, wa_1=wa_1, wb_out=wb_out, wb_2=wb_2, wgwe=wgwe,
                gbe=gbe, wk_b=wk_b, wv_b=wv_b, wr_b=wr_b, wo_b=wo_b, cvec=cvec)


def _host_consts():
    f32 = np.float32
    return dict(
        iota1=np.tile(np.arange(512, dtype=f32), (128, 1)),
        iota8=np.tile(np.arange(8, dtype=f32), (128, 1)),
        egid=np.tile(np.repeat(np.array([0.0, 1.0], f32), 4), (128, 1)),
        ide_in=np.eye(128, dtype=f32),
        ones11=np.ones((1, 1), f32),
        onesc=np.ones((1, 128), f32),
    )


def kernel(**inputs):
    x = np.ascontiguousarray(np.asarray(inputs["L0_attn_x"], np.float32)
                             .reshape(4 * 8192, 1024))
    shared = _host_blocks(inputs)
    shared.update(_host_consts())
    in_maps = []
    for c in range(NCORES):
        seq, half = divmod(c, 2)
        r0 = seq * 8192 + half * NTOK
        pm = np.zeros((8, 1), np.float32)
        pmr = np.zeros((128, 8), np.float32)
        if half == 1:
            pm[c - 1, 0] = 1.0
            pmr[:, c - 1] = 1.0
        m = dict(shared)
        m["x0"] = np.ascontiguousarray(x[r0:r0 + NTOK])
        m["pmask"] = pm
        m["pmaskr"] = pmr
        in_maps.append(m)
    nc = Kern(PHASES).build()
    res = run_bass_kernel_spmd(nc, in_maps, core_ids=list(range(NCORES)))
    out = np.empty((4, 8192, 1024), np.float32)
    for c in range(NCORES):
        seq, half = divmod(c, 2)
        out[seq, half * NTOK:(half + 1) * NTOK] = res.results[c]["out"]
    return out
